# version 05
# baseline (speedup 1.0000x reference)
import numpy as np
import concourse.bass as bass
import concourse.mybir as mybir
from concourse.bass_utils import run_bass_kernel_spmd
from contextlib import ExitStack

F32 = mybir.dt.float32
BF16 = mybir.dt.bfloat16
FP16 = mybir.dt.float16
FP8 = mybir.dt.float8e4
AF = mybir.ActivationFunctionType
ALU = mybir.AluOpType

D = 1024
KC = 8
TPC = 2048
NT = 4
TT = 512
H = 4096
HC = 32
NCORE = 8
EPS = 1e-5
LN32 = float(np.log(32.0))
LN1024_HALF = 0.5 * float(np.log(1024.0))

_CONSTS = [0.0, 1.0, -1.0, 0.5, -0.5, -0.125, 1.0 / 1024.0, LN32]

DEBUG_TAPS = False

_VEC_NAMES = ["g1", "g2", "blin", "bconv", "ba", "bx", "blin2", "b1", "b2",
              "b3", "lam", "fb2"]


class _Cnt:
    def __init__(self, handle):
        self.h = handle
        self.n = 0


def _build():
    nc = bass.Bass("TRN2", target_bir_lowering=False, debug=False)

    x_in = nc.dram_tensor("x", [D, TPC], F32, kind="ExternalInput").ap()
    w_in = {}
    for nm in ["wlin", "wconv", "wa", "wx", "wlin2", "w1a", "w2a", "w3a"]:
        w_in[nm] = nc.dram_tensor(nm, [D, D], F32, kind="ExternalInput").ap()
    fw1_in = nc.dram_tensor("fw1", [D, H], F32, kind="ExternalInput").ap()
    fw2_in = nc.dram_tensor("fw2", [H, D], F32, kind="ExternalInput").ap()
    v_in = {}
    for nm in _VEC_NAMES:
        v_in[nm] = nc.dram_tensor("v" + nm, [128, KC], F32,
                                  kind="ExternalInput").ap()
    v_in["fb1"] = nc.dram_tensor("vfb1", [128, HC], F32,
                                 kind="ExternalInput").ap()
    v_in["mask"] = nc.dram_tensor("vmask", [128, KC], F32,
                                  kind="ExternalInput").ap()

    x_out = nc.dram_tensor("xout", [D, TPC], F32, kind="ExternalOutput").ap()

    dbg = {}
    if DEBUG_TAPS:
        for nm, dt_ in [("d_xgh", BF16), ("d_res2", F32), ("d_xm", BF16),
                        ("d_comb", BF16)]:
            dbg[nm] = nc.dram_tensor(nm, [D, TPC], dt_,
                                     kind="ExternalOutput").ap() \
                .rearrange("(c p) t -> p c t", p=128)

    s_ln1 = nc.dram_tensor("s_ln1", [D, TPC], F32, kind="Internal").ap()
    s_la = nc.dram_tensor("s_la", [D, TPC], BF16, kind="Internal").ap()
    s_g = nc.dram_tensor("s_g", [D, TPC], F32, kind="Internal").ap()
    s_xg = nc.dram_tensor("s_xg", [D, TPC], BF16, kind="Internal").ap()
    s_x1 = nc.dram_tensor("s_x1", [D, TPC], F32, kind="Internal").ap()
    cc_in = nc.dram_tensor("cc_in", [128, KC], F32, kind="Internal").ap()
    cc_out = nc.dram_tensor("cc_out", [2, 128, KC], F32, kind="Internal").ap()

    xr_in = x_in.rearrange("(c p) t -> p c t", p=128)
    xr_out = x_out.rearrange("(c p) t -> p c t", p=128)
    wr_in = {nm: a.rearrange("(c p) o -> p c o", p=128)
             for nm, a in w_in.items()}
    fw1_r = fw1_in.rearrange("(c p) o -> p c o", p=128)
    fw2_r = fw2_in.rearrange("(c p) o -> p c o", p=128)
    sr_ln1 = s_ln1.rearrange("(c p) t -> p c t", p=128)
    sr_la = s_la.rearrange("(c p) t -> p c t", p=128)
    sr_g = s_g.rearrange("(c p) t -> p c t", p=128)
    sr_xg = s_xg.rearrange("(c p) t -> p c t", p=128)
    sr_x1 = s_x1.rearrange("(c p) t -> p c t", p=128)

    with ExitStack() as top:
        def sb(name, shape, dt_, stack):
            return stack.enter_context(nc.sbuf_tensor(name, shape, dt_))

        cpool = sb("cpool", [128, len(_CONSTS)], F32, top)
        for i, val in enumerate(_CONSTS):
            nc.const_aps.register(F32, val, cpool[:, i:i + 1])
        ones_bf = sb("ones_bf", [128, 1], BF16, top)
        ones_f32 = sb("ones_f32", [128, 1], F32, top)
        ones_f = sb("ones_f", [1, 128], F32, top)
        vec = {nm: sb("sv_" + nm, [128, KC], F32, top) for nm in _VEC_NAMES}
        vec["fb1"] = sb("sv_fb1", [128, HC], F32, top)
        vec["mask"] = sb("sv_mask", [128, KC], F32, top)
        logaC = sb("logaC", [128, KC], F32, top)
        lam_t = sb("lam_t", [128, KC], F32, top)
        bq16 = sb("bq16", [128, KC], FP16, top)
        bq32 = sb("bq32", [128, KC], F32, top)
        carry = sb("carry", [128, KC], F32, top)
        hraw = sb("hraw", [128, KC], F32, top)
        rows_t = sb("rows_t", [1, 6 * TT], F32, top)

        def row(i):
            return rows_t[0:1, i * TT:(i + 1) * TT]

        psS1 = top.enter_context(nc.psum_tensor("psS1", [128, TT], F32))
        psS2 = top.enter_context(nc.psum_tensor("psS2", [128, TT], F32))
        psB = top.enter_context(nc.psum_tensor("psB", [128, 3 * TT], F32))
        psM0 = top.enter_context(nc.psum_tensor("psM0", [128, TT], F32))
        psM1 = top.enter_context(nc.psum_tensor("psM1", [128, TT], F32))
        psM = [psM0, psM1]

        sems = {nm: _Cnt(top.enter_context(nc.semaphore(nm)))
                for nm in ["dsy", "dgp", "sv", "ss", "st", "cc"]}
        dsy, dgp, sv, ss, st_, ccs = (sems[n] for n in
                                      ["dsy", "dgp", "sv", "ss", "st", "cc"])

        block = top.enter_context(nc.Block())
        SY, GP, V, S, T = (block.sync, block.gpsimd, block.vector,
                           block.scalar, block.tensor)

        def emit(engine_sel, waits, fn, inc=None, inc_k=1):
            def body(eng):
                for cnt, val in waits:
                    if val > 0:
                        eng.wait_ge(cnt.h, val)
                last = fn(eng)
                if inc is not None:
                    assert last is not None
                    last.then_inc(inc.h, inc_k)
            engine_sel(body)
            if inc is not None:
                inc.n += inc_k

        def dma(qsel, out, in_, waits, cnt):
            emit(qsel, waits, lambda e: e.dma_start(out=out, in_=in_), cnt, 16)

        marks = {}

        for nm in _VEC_NAMES:
            dma(SY, vec[nm][:], v_in[nm], [], dsy)
        dma(SY, vec["fb1"][:], v_in["fb1"], [], dsy)
        dma(SY, vec["mask"][:], v_in["mask"], [], dsy)
        setup_loads = dsy.n

        def setup_v(v):
            for i, val in enumerate(_CONSTS):
                v.memset(cpool[:, i:i + 1], val)
            v.memset(ones_bf[:], 1.0)
            v.memset(ones_f32[:], 1.0)
            v.memset(ones_f[:], 1.0)
            return v.engine_nop()
        emit(V, [], setup_v, sv)
        consts_ready = sv.n

        def setup_s(s):
            s.wait_ge(sv.h, consts_ready)
            s.activation(lam_t[:], vec["lam"][:], AF.Exp, scale=-1.0)
            s.activation(logaC[:], lam_t[:], AF.Ln, bias=1.0)
            s.activation(logaC[:], logaC[:], AF.Copy, scale=-0.125)
            s.activation(bq16[:], vec["fb2"][:], AF.Copy)
            return s.activation(bq32[:], bq16[:], AF.Copy)
        emit(S, [(dsy, setup_loads)], setup_s, ss)
        setup_s_done = ss.n

        with ExitStack() as wsc:
            wslot = [sb(f"wslot{i}", [128, KC, D], BF16, wsc)
                     for i in range(4)]
            wA = {"wlin": wslot[0], "wconv": wslot[1],
                  "wa": wslot[2], "wx": wslot[3]}
            wB = {"wlin2": wslot[0], "w1a": wslot[1],
                  "w2a": wslot[2], "w3a": wslot[3]}
            for nm in ["wlin", "wconv", "wa", "wx"]:
                dma(GP, wA[nm][:], wr_in[nm], [], dgp)
            wA_done = {nm: 16 * (i + 1) for i, nm in
                       enumerate(["wlin", "wconv", "wa", "wx"])}

            pha = wsc.enter_context(ExitStack())
            x0t = sb("x0t", [128, KC, TT], F32, pha)
            sqA = sb("sqA", [128, KC, TT], BF16, pha)
            lin_bf = sb("lin_bf", [128, KC, TT], BF16, pha)
            ln1_t = sb("ln1_t", [128, KC, TT], F32, pha)
            a_t = sb("a_t", [128, KC, TT + 1], F32, pha)
            xn_t = sb("xn_t", [128, KC, TT], BF16, pha)
            la_t = xn_t
            xgel_t = sb("xgel_t", [128, KC, TT], BF16, pha)
            xrg_bf = sb("xrg_bf", [128, KC, TT], BF16, pha)
            it_bf = sb("it_bf", [128, KC, TT], BF16, pha)
            gated_t = sb("gated_t", [128, KC, TT + 1], F32, pha)
            tmpc = sb("tmpcA", [128, TT], F32, pha)
            sgc = sb("sgc", [128, TT], F32, pha)
            rt_c = [sb(f"rt{i}", [128, TT], F32, pha) for i in range(2)]
            h_c = sb("h_cA", [128, TT + 1], F32, pha)
            tmpc2 = h_c[:, 0:TT]

            def f_col0(v):
                for c in range(KC):
                    v.memset(a_t[:, c, 0:1], 0.0)
                    v.memset(gated_t[:, c, 0:1], 0.0)
                return v.engine_nop()
            emit(V, [], f_col0, sv)

            for j in range(NT):
                xsl = slice(j * TT, (j + 1) * TT)

                w = [] if j == 0 else [(sv, marks[("ln1", j - 1)])]
                dma(SY, x0t[:], xr_in[:, :, xsl], w, dsy)
                marks[("ldx", j)] = dsy.n

                w = [(dsy, marks[("ldx", j)]), (ss, setup_s_done)]
                if j > 0:
                    w.append((st_, marks[("statsA", j - 1)]))
                emit(S, w, lambda s: s.activation(sqA[:], x0t[:], AF.Square),
                     ss)
                marks[("sqA", j)] = ss.n

                def f_stats(t):
                    for c in range(KC):
                        t.matmul(psS1[0:1, :], ones_f32[:, 0:1], x0t[:, c, :],
                                 start=(c == 0), stop=(c == KC - 1))
                    last = None
                    for c in range(KC):
                        last = t.matmul(psS2[0:1, :], ones_bf[:, 0:1],
                                        sqA[:, c, :], start=(c == 0),
                                        stop=(c == KC - 1))
                    return last
                w = [(dsy, marks[("ldx", j)]), (ss, marks[("sqA", j)]),
                     (sv, consts_ready)]
                if j > 0:
                    w.append((ss, marks[("rowsS", j - 1)]))
                emit(T, w, f_stats, st_)
                marks[("statsA", j)] = st_.n

                def f_rows1(s):
                    s.activation(row(0), psS1[0:1, :], AF.Copy,
                                 scale=1.0 / 1024.0)
                    return s.activation(row(1), psS2[0:1, :], AF.Copy,
                                        scale=1.0 / 1024.0)
                w = [(st_, marks[("statsA", j)])]
                if j > 0:
                    w.append((st_, marks[("bcastA", j - 1)]))
                emit(S, w, f_rows1, ss)
                marks[("rowsS", j)] = ss.n

                def f_rows2(v):
                    v.tensor_mul(row(2), row(0), row(0))
                    v.tensor_sub(row(3), row(1), row(2))
                    v.tensor_scalar_max(row(2), row(3), 1e-38)
                    return v.tensor_scalar_add(row(1), row(3), EPS)
                emit(V, [(ss, marks[("rowsS", j)])], f_rows2, sv)
                marks[("rowsV1", j)] = sv.n

                def f_rows3(s):
                    s.activation(row(4), row(2), AF.Ln)
                    s.activation(row(5), row(1), AF.Ln)
                    return s.activation(row(1), row(5), AF.Exp,
                                        scale=-0.5)
                emit(S, [(sv, marks[("rowsV1", j)])], f_rows3, ss)
                marks[("rowsS2", j)] = ss.n

                def f_rows4(v):
                    v.tensor_mul(row(2), row(0), row(1))
                    v.tensor_sub(row(3), row(5), row(4))
                    v.tensor_scalar(row(4), row(3), 0.5, -LN1024_HALF,
                                    ALU.mult, ALU.add)
                    return v.tensor_scalar_min(row(3), row(4), 27.631021)
                emit(V, [(ss, marks[("rowsS2", j)])], f_rows4, sv)
                marks[("rowsV2", j)] = sv.n

                def f_rows5(s):
                    return s.activation(row(4), row(3), AF.Exp, bias=LN32)
                emit(S, [(sv, marks[("rowsV2", j)])], f_rows5, ss)
                marks[("rowsS3", j)] = ss.n

                def f_bcast(t):
                    t.matmul(psB[:, 0:TT], ones_f[0:1, :], row(1),
                             start=True, stop=True)
                    t.matmul(psB[:, TT:2 * TT], ones_f[0:1, :], row(2),
                             start=True, stop=True)
                    return t.matmul(psB[:, 2 * TT:3 * TT], ones_f[0:1, :],
                                    row(4), start=True, stop=True)
                w = [(ss, marks[("rowsS3", j)])]
                if j > 0:
                    w.append((sv, marks[("ln1", j - 1)]))
                emit(T, w, f_bcast, st_)
                marks[("bcastA", j)] = st_.n

                def f_ln1(v):
                    for c in range(KC):
                        v.tensor_mul(tmpc[:], x0t[:, c, :], psB[:, 0:TT])
                        v.tensor_sub(ln1_t[:, c, :], tmpc[:],
                                     psB[:, TT:2 * TT])
                        v.scalar_tensor_tensor(xn_t[:, c, :],
                                               ln1_t[:, c, :],
                                               vec["g1"][:, c:c + 1],
                                               psB[:, 2 * TT:3 * TT],
                                               ALU.mult, ALU.mult)
                    return v.engine_nop()
                w = [(st_, marks[("bcastA", j)])]
                if j > 0:
                    w += [(dsy, marks[("dma_ln1", j - 1)]),
                          (dsy, marks[("dma_la", j - 1)]),
                          (st_, marks[("linmm", j - 1)])]
                emit(V, w, f_ln1, sv)
                marks[("ln1", j)] = sv.n

                dma(SY, sr_ln1[:, :, xsl], ln1_t[:],
                    [(sv, marks[("ln1", j)])], dsy)
                marks[("dma_ln1", j)] = dsy.n

                def mm8(wsb, rhs, evbase_mark, per, sem):
                    def f(t):
                        for odc in range(KC):
                            if odc >= 2:
                                t.wait_ge(sem.h, marks[evbase_mark] +
                                          per * (odc - 1))
                            last = None
                            for kc in range(KC):
                                last = t.matmul(
                                    psM[odc % 2][:],
                                    wsb[:, kc, odc * 128:(odc + 1) * 128],
                                    rhs[:, kc, :],
                                    start=(kc == 0), stop=(kc == KC - 1))
                            last.then_inc(st_.h, 1)
                            st_.n += 1
                        return None
                    return f

                marks[("linev_base", j)] = ss.n
                w = [(sv, marks[("ln1", j)]), (dgp, wA_done["wlin"])]
                if j > 0:
                    w.append((ss, marks[("it", j - 1)]))
                emit(T, w, mm8(wA["wlin"], xn_t, ("linev_base", j), 2, ss),
                     None)
                marks[("linmm", j)] = st_.n

                def f_linev(s, j=j):
                    base = marks[("linmm", j)] - KC
                    for odc in range(KC):
                        s.wait_ge(st_.h, base + odc + 1)
                        s.activation(lin_bf[:, odc, :], psM[odc % 2][:],
                                     AF.Identity,
                                     bias=vec["blin"][:, odc:odc + 1])
                        last = s.activation(xgel_t[:, odc, :], psM[odc % 2][:],
                                            AF.Gelu,
                                            bias=vec["blin"][:, odc:odc + 1])
                        last.then_inc(ss.h, 2)
                        ss.n += 2
                    return None
                w = [(st_, marks[("statsA", j)])]
                if j > 0:
                    w += [(st_, marks[("xrgmm", j - 1)]),
                          (dsy, marks[("dma_xg", j - 1)])]
                emit(S, w, f_linev, None)
                marks[("xgel", j)] = ss.n

                dma(SY, sr_xg[:, :, xsl], xgel_t[:],
                    [(ss, marks[("xgel", j)])], dsy)
                marks[("dma_xg", j)] = dsy.n

                marks[("xrgev_base", j)] = ss.n
                emit(T, [(ss, marks[("xgel", j)]), (dgp, wA_done["wconv"])],
                     mm8(wA["wconv"], lin_bf, ("xrgev_base", j), 1, ss),
                     None)
                marks[("xrgmm", j)] = st_.n

                def f_xrgev(s, j=j):
                    base = marks[("xrgmm", j)] - KC
                    for odc in range(KC):
                        s.wait_ge(st_.h, base + odc + 1)
                        last = s.activation(xrg_bf[:, odc, :], psM[odc % 2][:],
                                            AF.Identity,
                                            bias=vec["bconv"][:, odc:odc + 1])
                        last.then_inc(ss.h, 1)
                        ss.n += 1
                    return None
                w = []
                if j > 0:
                    w.append((st_, marks[("itmm", j - 1)]))
                emit(S, w, f_xrgev, None)
                marks[("xrg", j)] = ss.n

                marks[("rtev_base", j)] = ss.n
                emit(T, [(ss, marks[("xrg", j)]), (dgp, wA_done["wa"])],
                     mm8(wA["wa"], xrg_bf, ("rtev_base", j), 1, ss), None)
                marks[("rtmm", j)] = st_.n

                marks[("la_base", j)] = sv.n

                def f_rtev(s, j=j):
                    base = marks[("rtmm", j)] - KC
                    for odc in range(KC):
                        s.wait_ge(st_.h, base + odc + 1)
                        if odc >= 2:
                            s.wait_ge(sv.h, marks[("la_base", j)] + odc - 1)
                        last = s.activation(rt_c[odc % 2][:], psM[odc % 2][:],
                                            AF.Sigmoid,
                                            bias=vec["ba"][:, odc:odc + 1])
                        last.then_inc(ss.h, 1)
                        ss.n += 1
                    return None
                emit(S, [(st_, marks[("linmm", j)])], f_rtev, None)
                marks[("rt", j)] = ss.n

                def f_la(v, j=j):
                    base = marks[("rt", j)] - KC
                    for odc in range(KC):
                        v.wait_ge(ss.h, base + odc + 1)
                        last = v.tensor_scalar(la_t[:, odc, :],
                                               rt_c[odc % 2][:],
                                               logaC[:, odc:odc + 1], None,
                                               ALU.mult)
                        last.then_inc(sv.h, 1)
                        sv.n += 1
                    return None
                emit(V, [], f_la, None)
                marks[("la", j)] = sv.n

                dma(SY, sr_la[:, :, xsl], la_t[:],
                    [(sv, marks[("la", j)])], dsy)
                marks[("dma_la", j)] = dsy.n

                marks[("itev_base", j)] = ss.n
                emit(T, [(ss, marks[("rt", j)]), (dgp, wA_done["wx"])],
                     mm8(wA["wx"], xrg_bf, ("itev_base", j), 1, ss), None)
                marks[("itmm", j)] = st_.n

                def f_itev(s, j=j):
                    base = marks[("itmm", j)] - KC
                    for odc in range(KC):
                        s.wait_ge(st_.h, base + odc + 1)
                        last = s.activation(it_bf[:, odc, :], psM[odc % 2][:],
                                            AF.Sigmoid,
                                            bias=vec["bx"][:, odc:odc + 1])
                        last.then_inc(ss.h, 1)
                        ss.n += 1
                    return None
                w = []
                if j > 0:
                    w.append((sv, marks[("gated", j - 1)]))
                emit(S, w, f_itev, None)
                marks[("it", j)] = ss.n

                w = [(sv, marks[("la", j)]), (dsy, marks[("dma_la", j)])]
                if j > 0:
                    w.append((sv, marks[("gated", j - 1)]))
                emit(S, w,
                     lambda s: s.activation(a_t[:, :, 1:TT + 1], la_t[:],
                                            AF.Exp), ss)
                marks[("a", j)] = ss.n

                def f_om(v):
                    for c in range(KC):
                        v.tensor_mul(tmpc[:], a_t[:, c, 1:TT + 1],
                                     a_t[:, c, 1:TT + 1])
                        v.tensor_scalar(tmpc2, tmpc[:], -1.0, 1.0,
                                        ALU.mult, ALU.add)
                        v.tensor_scalar_max(gated_t[:, c, 1:TT + 1], tmpc2,
                                            1e-38)
                    return v.engine_nop()
                w = [(ss, marks[("a", j)])]
                if j > 0:
                    w.append((dsy, marks[("dma_g", j - 1)]))
                emit(V, w, f_om, sv)
                marks[("om", j)] = sv.n

                def f_sg(s, j=j):
                    for c in range(KC):
                        s.activation(sgc[:], gated_t[:, c, 1:TT + 1], AF.Ln)
                        last = s.activation(gated_t[:, c, 1:TT + 1], sgc[:],
                                            AF.Exp, scale=0.5)
                        last.then_inc(ss.h, 1)
                        ss.n += 1
                    return None
                emit(S, [(sv, marks[("om", j)])], f_sg, None)
                marks[("sg", j)] = ss.n

                def f_gsc(v, j=j):
                    for c in range(KC):
                        v.wait_ge(ss.h, marks[("sg", j)] - KC + c + 1)
                        v.tensor_mul(tmpc[:], gated_t[:, c, 1:TT + 1],
                                     it_bf[:, c, :])
                        v.tensor_mul(gated_t[:, c, 1:TT + 1], tmpc[:],
                                     xrg_bf[:, c, :])
                        v.tensor_tensor_scan(h_c[:], a_t[:, c, 0:TT + 1],
                                             gated_t[:, c, 0:TT + 1], 0.0,
                                             ALU.mult, ALU.add) \
                            .then_inc(sv.h, 1)
                        sv.n += 1
                        v.wait_ge(sv.h, sv.n)
                        v.tensor_copy(gated_t[:, c, 0:1], h_c[:, TT:TT + 1])
                        if j == NT - 1:
                            v.tensor_copy(carry[:, c:c + 1],
                                          h_c[:, TT:TT + 1])
                    return v.engine_nop()
                emit(V, [(ss, marks[("it", j)])], f_gsc, sv)
                marks[("gated", j)] = sv.n

                dma(SY, sr_g[:, :, xsl], gated_t[:, :, 1:TT + 1],
                    [(sv, marks[("gated", j)])], dsy)
                marks[("dma_g", j)] = dsy.n

            dma(SY, cc_in, carry[:], [(sv, marks[("gated", NT - 1)])], dsy)
            cc_up = dsy.n

            emit(GP, [(dsy, cc_up)],
                 lambda g: g.collective_compute(
                     "AllGather", ALU.bypass,
                     replica_groups=[[0, 1], [2, 3], [4, 5], [6, 7]],
                     ins=[cc_in], outs=[cc_out]), ccs)

            dma(SY, hraw[:], cc_out[0], [(ccs, 1)], dsy)
            hraw_ld = dsy.n

            for nm, amark in [("wlin2", ("linmm", NT - 1)),
                              ("w1a", ("xrgmm", NT - 1)),
                              ("w2a", ("rtmm", NT - 1)),
                              ("w3a", ("itmm", NT - 1))]:
                emit(GP, [(st_, marks[amark])],
                     lambda g, nm=nm: g.dma_start(out=wB[nm][:],
                                                  in_=wr_in[nm]), dgp, 16)
                marks[("wB", nm)] = dgp.n

            a_end_st, a_end_ss, a_end_sv, a_end_dsy = (st_.n, ss.n, sv.n,
                                                       dsy.n)
            pha.close()

            phb = wsc.enter_context(ExitStack())
            laB = sb("laB", [128, KC, TT], BF16, phb)
            gatB = sb("gatB", [128, KC, TT + 1], F32, phb)
            xgB = sb("xgB", [128, KC, TT], BF16, phb)
            aB = sb("aB", [128, KC, TT + 1], F32, phb)
            outs_t = aB
            gateB = sb("gateB", [128, KC, TT], BF16, phb)
            xghB = sb("xghB", [128, KC, TT], BF16, phb)
            res2 = sb("res2", [128, KC, TT], F32, phb)
            sqB = sb("sqB", [128, TT], BF16, phb)
            xmB = sb("xmB", [128, KC, TT], BF16, phb)
            combB = sb("combB", [128, KC, TT], BF16, phb)
            hB = sb("hB", [128, TT + 1], F32, phb)
            tvB = [sb(f"tvB{i}", [128, TT], F32, phb) for i in range(2)]
            x1B = [sb(f"x1B{i}", [128, TT], F32, phb) for i in range(2)]
            ln1B = sb("ln1B", [128, KC, TT], F32, phb)
            x0B = ln1B

            aend = [(sv, a_end_sv), (ss, a_end_ss), (st_, a_end_st)]

            def f_bcol0(v):
                for c in range(KC):
                    v.memset(aB[:, c, 0:1], 0.0)
                    v.tensor_mul(gatB[:, c, 0:1], hraw[:, c:c + 1],
                                 vec["mask"][:, c:c + 1])
                return v.engine_nop()
            emit(V, [(dsy, hraw_ld)] + aend, f_bcol0, sv)
            bcol0_ready = sv.n

            for j in range(NT):
                xsl = slice(j * TT, (j + 1) * TT)
                w = list(aend) if j == 0 else [(ss, marks[("aB", j - 1)])]
                dma(SY, laB[:], sr_la[:, :, xsl], w, dsy)
                marks[("ld_laB", j)] = dsy.n
                w = list(aend) if j == 0 else [(sv, marks[("scanB", j - 1)])]
                dma(SY, gatB[:, :, 1:TT + 1], sr_g[:, :, xsl], w, dsy)
                marks[("ld_gB", j)] = dsy.n
                w = list(aend) if j == 0 else [(sv, marks[("x1", j - 1)])]
                dma(SY, ln1B[:], sr_ln1[:, :, xsl], w, dsy)
                marks[("ld_ln1B", j)] = dsy.n
                w = list(aend) if j == 0 else [(sv, marks[("scanB", j - 1)])]
                dma(SY, xgB[:], sr_xg[:, :, xsl], w, dsy)
                marks[("ld_xgB", j)] = dsy.n

                w = [(dsy, marks[("ld_laB", j)])]
                if j > 0:
                    w.append((ss, marks[("gate", j - 1)]))
                else:
                    w.append((st_, a_end_st))
                emit(S, w, lambda s: s.activation(aB[:, :, 1:TT + 1],
                                                  laB[:], AF.Exp), ss)
                marks[("aB", j)] = ss.n

                def f_scan(v, j=j):
                    for c in range(KC):
                        v.tensor_tensor_scan(hB[:], aB[:, c, 0:TT + 1],
                                             gatB[:, c, 0:TT + 1], 0.0,
                                             ALU.mult, ALU.add) \
                            .then_inc(sv.h, 1)
                        sv.n += 1
                        v.wait_ge(sv.h, sv.n)
                        v.tensor_mul(xghB[:, c, :], hB[:, 1:TT + 1],
                                     xgB[:, c, :])
                        v.tensor_copy(gatB[:, c, 0:1], hB[:, TT:TT + 1])
                    return v.engine_nop()
                w = [(ss, marks[("aB", j)]), (dsy, marks[("ld_gB", j)]),
                     (dsy, marks[("ld_xgB", j)])]
                if j == 0:
                    w.append((sv, bcol0_ready))
                else:
                    w.append((st_, marks[("xrmm", j - 1)]))
                emit(V, w, f_scan, sv)
                marks[("scanB", j)] = sv.n

                if DEBUG_TAPS:
                    dma(SY, dbg["d_xgh"][:, :, xsl], xghB[:],
                        [(sv, marks[("scanB", j)])], dsy)

                marks[("res2_base", j)] = sv.n
                w = [(sv, marks[("scanB", j)]), (dgp, marks[("wB", "wlin2")]),
                     (dsy, marks[("ld_ln1B", j)])]
                if j > 0:
                    w.append((sv, marks[("x1", j - 1)]))
                emit(T, w, mm8(wB["wlin2"], xghB, ("res2_base", j), 1, sv),
                     None)
                marks[("xrmm", j)] = st_.n

                def f_res2(v, j=j):
                    base = marks[("xrmm", j)] - KC
                    for odc in range(KC):
                        v.wait_ge(st_.h, base + odc + 1)
                        last = v.scalar_tensor_tensor(
                            res2[:, odc, :], psM[odc % 2][:],
                            vec["blin2"][:, odc:odc + 1], ln1B[:, odc, :],
                            ALU.add, ALU.add)
                        last.then_inc(sv.h, 1)
                        sv.n += 1
                    return None
                emit(V, [(dsy, marks[("ld_ln1B", j)])], f_res2, None)
                marks[("res2", j)] = sv.n
                if DEBUG_TAPS:
                    dma(SY, dbg["d_res2"][:, :, xsl], res2[:],
                        [(sv, marks[("res2", j)])], dsy)

                marks[("sqB_base", j)] = ss.n
                marks[("statsB_base", j)] = st_.n

                def f_sqB(s, j=j):
                    base = marks[("res2", j)] - KC
                    for c in range(KC):
                        s.wait_ge(sv.h, base + c + 1)
                        if c >= 1:
                            s.wait_ge(st_.h,
                                      marks[("statsB_base", j)] + c)
                        last = s.activation(sqB[:], res2[:, c, :], AF.Square)
                        last.then_inc(ss.h, 1)
                        ss.n += 1
                    return None
                emit(S, [], f_sqB, None)
                marks[("sqB", j)] = ss.n

                def f_statsB(t, j=j):
                    base = marks[("sqB_base", j)]
                    for c in range(KC):
                        t.wait_ge(ss.h, base + c + 1)
                        last = t.matmul(psS2[0:1, :], ones_bf[:, 0:1],
                                        sqB[:], start=(c == 0),
                                        stop=(c == KC - 1))
                        last.then_inc(st_.h, 1)
                        st_.n += 1
                    return None
                w = []
                if j > 0:
                    w.append((ss, marks[("rowsB1", j - 1)]))
                emit(T, w, f_statsB, None)
                marks[("statsB", j)] = st_.n

                emit(S, [(st_, marks[("statsB", j)])],
                     lambda s: s.activation(row(0), psS2[0:1, :], AF.Copy),
                     ss)
                marks[("rowsB1", j)] = ss.n

                emit(V, [(ss, marks[("rowsB1", j)])],
                     lambda v: v.tensor_scalar_max(row(1), row(0), 1e-24), sv)
                marks[("rowsB2", j)] = sv.n

                def f_rowsB3(s):
                    s.activation(row(2), row(1), AF.Ln)
                    return s.activation(row(3), row(2), AF.Exp, scale=-0.5,
                                        bias=LN32)
                emit(S, [(sv, marks[("rowsB2", j)])], f_rowsB3, ss)
                marks[("rowsB3", j)] = ss.n

                w = [(ss, marks[("rowsB3", j)])]
                w.append((sv, a_end_sv) if j == 0
                         else (sv, marks[("xm", j - 1)]))
                emit(T, w,
                     lambda t: t.matmul(psB[:, 0:TT], ones_f[0:1, :], row(3),
                                        start=True, stop=True), st_)
                marks[("bcastB", j)] = st_.n

                def f_xm(v):
                    for c in range(KC):
                        v.scalar_tensor_tensor(xmB[:, c, :], res2[:, c, :],
                                               vec["g2"][:, c:c + 1],
                                               psB[:, 0:TT],
                                               ALU.mult, ALU.mult)
                    return v.engine_nop()
                w = [(st_, marks[("bcastB", j)])]
                if j > 0:
                    w.append((st_, marks[("w2mm", j - 1)]))
                emit(V, w, f_xm, sv)
                marks[("xm", j)] = sv.n
                if DEBUG_TAPS:
                    dma(SY, dbg["d_xm"][:, :, xsl], xmB[:],
                        [(sv, marks[("xm", j)])], dsy)

                marks[("outs_base", j)] = ss.n
                emit(T, [(sv, marks[("xm", j)]), (dgp, marks[("wB", "w1a")])],
                     mm8(wB["w1a"], xmB, ("outs_base", j), 1, ss), None)
                marks[("w1mm", j)] = st_.n

                def f_outs(s, j=j):
                    base = marks[("w1mm", j)] - KC
                    for odc in range(KC):
                        s.wait_ge(st_.h, base + odc + 1)
                        last = s.activation(outs_t[:, odc, 1:TT + 1],
                                            psM[odc % 2][:], AF.Sigmoid,
                                            bias=vec["b1"][:, odc:odc + 1])
                        last.then_inc(ss.h, 1)
                        ss.n += 1
                    return None
                emit(S, [(sv, marks[("scanB", j)])], f_outs, None)
                marks[("outs", j)] = ss.n

                w = [(ss, marks[("outs", j)])]
                if j > 0:
                    w.append((sv, marks[("comb", j - 1)]))
                emit(S, w,
                     lambda s: s.activation(gateB[:], outs_t[:, :, 1:TT + 1],
                                            AF.Gelu), ss)
                marks[("gate", j)] = ss.n

                marks[("comb_base", j)] = sv.n
                emit(T, [(ss, marks[("gate", j)]),
                         (dgp, marks[("wB", "w2a")])],
                     mm8(wB["w2a"], xmB, ("comb_base", j), 1, sv), None)
                marks[("w2mm", j)] = st_.n

                def f_comb(v, j=j):
                    base = marks[("w2mm", j)] - KC
                    for odc in range(KC):
                        v.wait_ge(st_.h, base + odc + 1)
                        last = v.scalar_tensor_tensor(
                            combB[:, odc, :], psM[odc % 2][:],
                            vec["b2"][:, odc:odc + 1], gateB[:, odc, :],
                            ALU.add, ALU.mult)
                        last.then_inc(sv.h, 1)
                        sv.n += 1
                    return None
                emit(V, [], f_comb, None)
                marks[("comb", j)] = sv.n
                if DEBUG_TAPS:
                    dma(SY, dbg["d_comb"][:, :, xsl], combB[:],
                        [(sv, marks[("comb", j)])], dsy)

                dma(SY, x0B[:], xr_in[:, :, xsl], [(sv, marks[("res2", j)])],
                    dsy)
                marks[("ld_x0B", j)] = dsy.n

                marks[("x1_base", j)] = sv.n
                emit(T, [(sv, marks[("comb", j)]),
                         (dgp, marks[("wB", "w3a")])],
                     mm8(wB["w3a"], combB, ("x1_base", j), 2, sv), None)
                marks[("w3mm", j)] = st_.n

                marks[("x1dma_base", j)] = dsy.n

                def f_x1(v, j=j):
                    base = marks[("w3mm", j)] - KC
                    for odc in range(KC):
                        v.wait_ge(st_.h, base + odc + 1)
                        if odc >= 2:
                            v.wait_ge(dsy.h, marks[("x1dma_base", j)] +
                                      16 * (odc - 1))
                        v.scalar_tensor_tensor(tvB[odc % 2][:],
                                               psM[odc % 2][:],
                                               vec["b3"][:, odc:odc + 1],
                                               res2[:, odc, :],
                                               ALU.add, ALU.add)
                        last = v.tensor_add(x1B[odc % 2][:], tvB[odc % 2][:],
                                            x0B[:, odc, :])
                        last.then_inc(sv.h, 2)
                        sv.n += 2
                    return None
                emit(V, [(dsy, marks[("ld_x0B", j)])], f_x1, None)
                marks[("x1", j)] = sv.n

                def f_x1dma(sync, j=j):
                    base = marks[("x1", j)] - 2 * KC
                    for odc in range(KC):
                        sync.wait_ge(sv.h, base + 2 * (odc + 1))
                        last = sync.dma_start(out=sr_x1[:, odc, xsl],
                                              in_=x1B[odc % 2][:])
                        last.then_inc(dsy.h, 16)
                        dsy.n += 16
                    return None
                emit(SY, [], f_x1dma, None)
                marks[("x1dma", j)] = dsy.n

            b_end_st, b_end_ss, b_end_sv, b_end_dsy = (st_.n, ss.n, sv.n,
                                                       dsy.n)

        with ExitStack() as phc:
            w1sb = sb("w1sb", [128, KC, H], BF16, phc)
            w2sb = sb("w2sb", [128, HC, D], FP8, phc)
            first = True
            for q in range(4):
                w = [(st_, b_end_st), (sv, b_end_sv), (ss, b_end_ss),
                     (dsy, b_end_dsy)] if first else []
                first = False
                emit(GP, w,
                     lambda g, q=q: g.dma_start(
                         out=w1sb[:, :, q * 1024:(q + 1) * 1024],
                         in_=fw1_r[:, :, q * 1024:(q + 1) * 1024]),
                     dgp, 16)
                marks[("w1q", q)] = dgp.n
            emit(GP, [], lambda g: g.dma_start(out=w2sb[:], in_=fw2_r),
                 dgp, 16)
            w2_done = dgp.n

            x1C = [sb(f"x1C{i}", [128, KC, TT], F32, phc) for i in range(2)]
            sqC = sb("sqC", [128, KC, TT], BF16, phc)
            ln2C = sb("ln2C", [128, KC, TT], BF16, phc)
            hqC = sb("hqC", [128, HC, TT], FP8, phc)
            o16 = [sb(f"o16_{i}", [128, TT], FP16, phc) for i in range(2)]
            xoC = [sb(f"xoC{i}", [128, TT], F32, phc) for i in range(2)]
            tmpC = sb("tmpC", [128, TT], F32, phc)

            for j in range(NT):
                xsl = slice(j * TT, (j + 1) * TT)
                x1t = x1C[j % 2]
                w = [(dsy, marks[("x1dma", j)])]
                if j > 1:
                    w.append((sv, marks[("xoutV", j - 2)]))
                dma(SY, x1t[:], sr_x1[:, :, xsl], w, dsy)
                marks[("ld_x1C", j)] = dsy.n

                w = [(dsy, marks[("ld_x1C", j)])]
                if j > 0:
                    w.append((st_, marks[("statsC", j - 1)]))
                emit(S, w, lambda s, x1t=x1t: s.activation(sqC[:], x1t[:],
                                                           AF.Square), ss)
                marks[("sqC", j)] = ss.n

                def f_statsC(t, x1t=x1t):
                    for c in range(KC):
                        t.matmul(psS1[0:1, :], ones_f32[:, 0:1], x1t[:, c, :],
                                 start=(c == 0), stop=(c == KC - 1))
                    last = None
                    for c in range(KC):
                        last = t.matmul(psS2[0:1, :], ones_bf[:, 0:1],
                                        sqC[:, c, :], start=(c == 0),
                                        stop=(c == KC - 1))
                    return last
                w = [(dsy, marks[("ld_x1C", j)]), (ss, marks[("sqC", j)])]
                w.append((ss, b_end_ss) if j == 0
                         else (ss, marks[("rowsC1", j - 1)]))
                emit(T, w, f_statsC, st_)
                marks[("statsC", j)] = st_.n

                def f_rowsC1(s):
                    s.activation(row(0), psS1[0:1, :], AF.Copy,
                                 scale=1.0 / 1024.0)
                    return s.activation(row(1), psS2[0:1, :], AF.Copy,
                                        scale=1.0 / 1024.0)
                emit(S, [(st_, marks[("statsC", j)])], f_rowsC1, ss)
                marks[("rowsC1", j)] = ss.n

                def f_rowsC2(v):
                    v.tensor_mul(row(2), row(0), row(0))
                    v.tensor_sub(row(3), row(1), row(2))
                    return v.tensor_scalar_add(row(2), row(3), EPS)
                emit(V, [(ss, marks[("rowsC1", j)])], f_rowsC2, sv)
                marks[("rowsC2", j)] = sv.n

                def f_rowsC3(s):
                    s.activation(row(4), row(2), AF.Ln)
                    return s.activation(row(5), row(4), AF.Exp, scale=-0.5)
                emit(S, [(sv, marks[("rowsC2", j)])], f_rowsC3, ss)
                marks[("rowsC3", j)] = ss.n

                emit(V, [(ss, marks[("rowsC3", j)])],
                     lambda v: v.tensor_mul(row(1), row(0), row(5)), sv)
                marks[("rowsC4", j)] = sv.n

                def f_bcastC(t):
                    t.matmul(psB[:, 0:TT], ones_f[0:1, :], row(5),
                             start=True, stop=True)
                    return t.matmul(psB[:, TT:2 * TT], ones_f[0:1, :], row(1),
                                    start=True, stop=True)
                w = [(sv, marks[("rowsC4", j)])]
                w.append((sv, b_end_sv) if j == 0
                         else (ss, marks[("o16", j - 1)]))
                emit(T, w, f_bcastC, st_)
                marks[("bcastC", j)] = st_.n

                def f_ln2(v, x1t=x1t):
                    for c in range(KC):
                        v.tensor_mul(tmpC[:], x1t[:, c, :], psB[:, 0:TT])
                        v.tensor_sub(ln2C[:, c, :], tmpC[:],
                                     psB[:, TT:2 * TT])
                    return v.engine_nop()
                w = [(st_, marks[("bcastC", j)])]
                if j > 0:
                    w.append((st_, marks[("w1mmC", j - 1)]))
                emit(V, w, f_ln2, sv)
                marks[("ln2", j)] = sv.n

                marks[("hq_base", j)] = ss.n

                def f_w1(t, j=j):
                    for hc in range(HC):
                        if hc >= 2:
                            t.wait_ge(ss.h, marks[("hq_base", j)] + hc - 1)
                        if hc == 8:
                            t.wait_ge(dgp.h, marks[("w1q", 1)])
                        if hc == 16:
                            t.wait_ge(dgp.h, marks[("w1q", 2)])
                        if hc == 24:
                            t.wait_ge(dgp.h, marks[("w1q", 3)])
                        last = None
                        for kc in range(KC):
                            last = t.matmul(
                                psM[hc % 2][:],
                                w1sb[:, kc, hc * 128:(hc + 1) * 128],
                                ln2C[:, kc, :],
                                start=(kc == 0), stop=(kc == KC - 1))
                        last.then_inc(st_.h, 1)
                        st_.n += 1
                    return None
                emit(T, [(sv, marks[("ln2", j)]), (dgp, marks[("w1q", 0)])],
                     f_w1, None)
                marks[("w1mmC", j)] = st_.n

                def f_hq(s, j=j):
                    base = marks[("w1mmC", j)] - HC
                    for hc in range(HC):
                        s.wait_ge(st_.h, base + hc + 1)
                        last = s.activation(hqC[:, hc, :], psM[hc % 2][:],
                                            AF.Relu,
                                            bias=vec["fb1"][:, hc:hc + 1])
                        last.then_inc(ss.h, 1)
                        ss.n += 1
                    return None
                w = []
                if j > 0:
                    w.append((st_, marks[("w2mmC", j - 1)]))
                emit(S, w, f_hq, None)
                marks[("hq", j)] = ss.n

                marks[("o16_base", j)] = ss.n

                def f_w2(t, j=j):
                    for odc in range(KC):
                        if odc >= 2:
                            t.wait_ge(ss.h, marks[("o16_base", j)] + odc - 1)
                        last = None
                        for hc in range(HC):
                            last = t.matmul(
                                psB[:, (odc % 2) * TT:(odc % 2 + 1) * TT],
                                w2sb[:, hc, odc * 128:(odc + 1) * 128],
                                hqC[:, hc, :],
                                start=(hc == 0), stop=(hc == HC - 1))
                        last.then_inc(st_.h, 1)
                        st_.n += 1
                    return None
                emit(T, [(ss, marks[("hq", j)]), (dgp, w2_done),
                         (sv, marks[("ln2", j)])], f_w2, None)
                marks[("w2mmC", j)] = st_.n

                marks[("xoutV_base", j)] = sv.n

                def f_o16(s, j=j):
                    base = marks[("w2mmC", j)] - KC
                    for odc in range(KC):
                        s.wait_ge(st_.h, base + odc + 1)
                        if odc >= 2:
                            s.wait_ge(sv.h, marks[("xoutV_base", j)] +
                                      odc - 1)
                        last = s.activation(
                            o16[odc % 2][:],
                            psB[:, (odc % 2) * TT:(odc % 2 + 1) * TT],
                            AF.Identity, bias=bq32[:, odc:odc + 1])
                        last.then_inc(ss.h, 1)
                        ss.n += 1
                    return None
                emit(S, [], f_o16, None)
                marks[("o16", j)] = ss.n

                marks[("xodma_base", j)] = dsy.n

                def f_xout(v, j=j, x1t=x1t):
                    base = marks[("o16", j)] - KC
                    for odc in range(KC):
                        v.wait_ge(ss.h, base + odc + 1)
                        if odc >= 2:
                            v.wait_ge(dsy.h, marks[("xodma_base", j)] +
                                      16 * (odc - 1))
                        last = v.tensor_add(xoC[odc % 2][:], o16[odc % 2][:],
                                            x1t[:, odc, :])
                        last.then_inc(sv.h, 1)
                        sv.n += 1
                    return None
                emit(V, [], f_xout, None)
                marks[("xoutV", j)] = sv.n

                def f_xodma(sync, j=j):
                    base = marks[("xoutV", j)] - KC
                    for odc in range(KC):
                        sync.wait_ge(sv.h, base + odc + 1)
                        last = sync.dma_start(out=xr_out[:, odc, xsl],
                                              in_=xoC[odc % 2][:])
                        last.then_inc(dsy.h, 16)
                        dsy.n += 16
                    return None
                emit(SY, [], f_xodma, None)
                marks[("xodma", j)] = dsy.n

            emit(SY, [(dsy, dsy.n)], lambda e: None, None)

    return nc


_CACHE = {}


def _get_prog():
    if "nc" not in _CACHE:
        _CACHE["nc"] = _build()
    return _CACHE["nc"]


def kernel(**inputs):
    inp = {k: np.asarray(v) for k, v in inputs.items()}

    def vstage(v, kc):
        v = np.asarray(v, np.float32)
        return np.ascontiguousarray(v.reshape(kc, 128).T)

    def wt(name):
        return np.ascontiguousarray(np.asarray(inp[name], np.float32).T)

    shared = {
        "wlin": wt("L0_attn_W_lin"), "wconv": wt("L0_attn_W_conv"),
        "wa": wt("L0_attn_Wa"), "wx": wt("L0_attn_Wx"),
        "wlin2": wt("L0_attn_W_lin2"), "w1a": wt("L0_attn_W1"),
        "w2a": wt("L0_attn_W2"), "w3a": wt("L0_attn_W3"),
        "fw1": wt("L0_ffn_w1"), "fw2": wt("L0_ffn_w2"),
        "vg1": vstage(inp["L0_attn_g1"], KC),
        "vg2": vstage(inp["L0_attn_g2"], KC),
        "vblin": vstage(inp["L0_attn_b_lin"], KC),
        "vbconv": vstage(inp["L0_attn_b_conv"], KC),
        "vba": vstage(inp["L0_attn_ba"], KC),
        "vbx": vstage(inp["L0_attn_bx"], KC),
        "vblin2": vstage(inp["L0_attn_b_lin2"], KC),
        "vb1": vstage(inp["L0_attn_b1"], KC),
        "vb2": vstage(inp["L0_attn_b2"], KC),
        "vb3": vstage(inp["L0_attn_b3"], KC),
        "vlam": vstage(inp["L0_attn_Lam"], KC),
        "vfb2": vstage(inp["L0_ffn_b2"], KC),
        "vfb1": vstage(inp["L0_ffn_b1"], HC),
    }

    x = np.asarray(inp["L0_attn_x"], np.float32)
    in_maps = []
    for c in range(NCORE):
        b, half = c // 2, c % 2
        t0 = half * TPC
        m = dict(shared)
        m["x"] = np.ascontiguousarray(x[b, t0:t0 + TPC, :].T)
        m["vmask"] = np.full((128, KC), float(half), np.float32)
        in_maps.append(m)

    nc = _get_prog()
    res = run_bass_kernel_spmd(nc, in_maps, core_ids=list(range(NCORE)))

    out = np.empty((4, 4096, D), np.float32)
    for c in range(NCORE):
        b, half = c // 2, c % 2
        t0 = half * TPC
        out[b, t0:t0 + TPC, :] = res.results[c]["xout"].T
    return out


# version 06
# speedup vs baseline: 1.1632x; 1.1632x over previous
import numpy as np
import concourse.bass as bass
import concourse.mybir as mybir
from concourse.bass_utils import run_bass_kernel_spmd
from contextlib import ExitStack

F32 = mybir.dt.float32
BF16 = mybir.dt.bfloat16
FP16 = mybir.dt.float16
FP8 = mybir.dt.float8e4
AF = mybir.ActivationFunctionType
ALU = mybir.AluOpType

D = 1024
KC = 8
TPC = 2048
NT = 4
TT = 512
H = 4096
HC = 32
NCORE = 8
EPS = 1e-5
LN32 = float(np.log(32.0))
LN1024_HALF = 0.5 * float(np.log(1024.0))

_CONSTS = [0.0, 1.0, -1.0, 0.5, -0.5, -0.125, 1.0 / 1024.0, LN32]

DEBUG_TAPS = False

_VEC_NAMES = ["g1", "g2", "blin", "bconv", "ba", "bx", "blin2", "b1", "b2",
              "b3", "lam", "fb2"]


class _Cnt:
    def __init__(self, handle):
        self.h = handle
        self.n = 0


def _build():
    nc = bass.Bass("TRN2", target_bir_lowering=False, debug=False)

    x_in = nc.dram_tensor("x", [D, TPC], F32, kind="ExternalInput").ap()
    w_in = {}
    for nm in ["wlin", "wconv", "wa", "wx", "wlin2", "w1a", "w2a", "w3a"]:
        w_in[nm] = nc.dram_tensor(nm, [D, D], F32, kind="ExternalInput").ap()
    fw1_in = nc.dram_tensor("fw1", [D, H], F32, kind="ExternalInput").ap()
    fw2_in = nc.dram_tensor("fw2", [H, D], F32, kind="ExternalInput").ap()
    v_in = {}
    for nm in _VEC_NAMES:
        v_in[nm] = nc.dram_tensor("v" + nm, [128, KC], F32,
                                  kind="ExternalInput").ap()
    v_in["fb1"] = nc.dram_tensor("vfb1", [128, HC], F32,
                                 kind="ExternalInput").ap()
    v_in["mask"] = nc.dram_tensor("vmask", [128, KC], F32,
                                  kind="ExternalInput").ap()

    x_out = nc.dram_tensor("xout", [D, TPC], F32, kind="ExternalOutput").ap()

    dbg = {}
    if DEBUG_TAPS:
        for nm, dt_ in [("d_xgh", BF16), ("d_res2", F32), ("d_xm", BF16),
                        ("d_comb", BF16)]:
            dbg[nm] = nc.dram_tensor(nm, [D, TPC], dt_,
                                     kind="ExternalOutput").ap() \
                .rearrange("(c p) t -> p c t", p=128)

    s_ln1 = nc.dram_tensor("s_ln1", [D, TPC], F32, kind="Internal").ap()
    s_la = nc.dram_tensor("s_la", [D, TPC], BF16, kind="Internal").ap()
    s_g = nc.dram_tensor("s_g", [D, TPC], F32, kind="Internal").ap()
    s_xg = nc.dram_tensor("s_xg", [D, TPC], BF16, kind="Internal").ap()
    s_x1 = nc.dram_tensor("s_x1", [D, TPC], F32, kind="Internal").ap()
    cc_in = nc.dram_tensor("cc_in", [128, KC], F32, kind="Internal").ap()
    cc_out = nc.dram_tensor("cc_out", [2, 128, KC], F32, kind="Internal").ap()

    xr_in = x_in.rearrange("(c p) t -> p c t", p=128)
    xr_out = x_out.rearrange("(c p) t -> p c t", p=128)
    wr_in = {nm: a.rearrange("(c p) o -> p c o", p=128)
             for nm, a in w_in.items()}
    fw1_r = fw1_in.rearrange("(c p) o -> p c o", p=128)
    fw2_r = fw2_in.rearrange("(c p) o -> p c o", p=128)
    sr_ln1 = s_ln1.rearrange("(c p) t -> p c t", p=128)
    sr_la = s_la.rearrange("(c p) t -> p c t", p=128)
    sr_g = s_g.rearrange("(c p) t -> p c t", p=128)
    sr_xg = s_xg.rearrange("(c p) t -> p c t", p=128)
    sr_x1 = s_x1.rearrange("(c p) t -> p c t", p=128)

    with ExitStack() as top:
        def sb(name, shape, dt_, stack):
            return stack.enter_context(nc.sbuf_tensor(name, shape, dt_))

        cpool = sb("cpool", [128, len(_CONSTS)], F32, top)
        for i, val in enumerate(_CONSTS):
            nc.const_aps.register(F32, val, cpool[:, i:i + 1])
        ones_bf = sb("ones_bf", [128, 1], BF16, top)
        ones_f32 = sb("ones_f32", [128, 1], F32, top)
        ones_f = sb("ones_f", [1, 128], F32, top)
        vec = {nm: sb("sv_" + nm, [128, KC], F32, top) for nm in _VEC_NAMES}
        vec["fb1"] = sb("sv_fb1", [128, HC], F32, top)
        vec["mask"] = sb("sv_mask", [128, KC], F32, top)
        logaC = sb("logaC", [128, KC], F32, top)
        lam_t = sb("lam_t", [128, KC], F32, top)
        bq16 = sb("bq16", [128, KC], FP16, top)
        bq32 = sb("bq32", [128, KC], F32, top)
        carry = sb("carry", [128, KC], F32, top)
        hraw = sb("hraw", [128, KC], F32, top)
        rows_t = sb("rows_t", [1, 6 * TT], F32, top)

        def row(i):
            return rows_t[0:1, i * TT:(i + 1) * TT]

        psS1 = top.enter_context(nc.psum_tensor("psS1", [128, TT], F32))
        psS2 = top.enter_context(nc.psum_tensor("psS2", [128, TT], F32))
        psB = top.enter_context(nc.psum_tensor("psB", [128, 3 * TT], F32))
        psM0 = top.enter_context(nc.psum_tensor("psM0", [128, TT], F32))
        psM1 = top.enter_context(nc.psum_tensor("psM1", [128, TT], F32))
        psM = [psM0, psM1]

        sems = {nm: _Cnt(top.enter_context(nc.semaphore(nm)))
                for nm in ["dsy", "dgp", "sv", "ss", "st", "cc"]}
        dsy, dgp, sv, ss, st_, ccs = (sems[n] for n in
                                      ["dsy", "dgp", "sv", "ss", "st", "cc"])

        block = top.enter_context(nc.Block())
        SY, GP, V, S, T = (block.sync, block.gpsimd, block.vector,
                           block.scalar, block.tensor)

        def emit(engine_sel, waits, fn, inc=None, inc_k=1):
            def body(eng):
                for cnt, val in waits:
                    if val > 0:
                        eng.wait_ge(cnt.h, val)
                last = fn(eng)
                if inc is not None:
                    assert last is not None
                    last.then_inc(inc.h, inc_k)
            engine_sel(body)
            if inc is not None:
                inc.n += inc_k

        def dma(qsel, out, in_, waits, cnt):
            emit(qsel, waits, lambda e: e.dma_start(out=out, in_=in_), cnt, 16)

        marks = {}

        for nm in _VEC_NAMES:
            dma(SY, vec[nm][:], v_in[nm], [], dsy)
        dma(SY, vec["fb1"][:], v_in["fb1"], [], dsy)
        dma(SY, vec["mask"][:], v_in["mask"], [], dsy)
        setup_loads = dsy.n

        def setup_v(v):
            for i, val in enumerate(_CONSTS):
                v.memset(cpool[:, i:i + 1], val)
            v.memset(ones_bf[:], 1.0)
            v.memset(ones_f32[:], 1.0)
            v.memset(ones_f[:], 1.0)
            return v.engine_nop()
        emit(V, [], setup_v, sv)
        consts_ready = sv.n

        def setup_s(s):
            s.wait_ge(sv.h, consts_ready)
            s.activation(lam_t[:], vec["lam"][:], AF.Exp, scale=-1.0)
            s.activation(logaC[:], lam_t[:], AF.Ln, bias=1.0)
            s.activation(logaC[:], logaC[:], AF.Copy, scale=-0.125)
            s.activation(bq16[:], vec["fb2"][:], AF.Copy)
            return s.activation(bq32[:], bq16[:], AF.Copy)
        emit(S, [(dsy, setup_loads)], setup_s, ss)
        setup_s_done = ss.n

        with ExitStack() as wsc:
            wslot = [sb(f"wslot{i}", [128, KC, D], BF16, wsc)
                     for i in range(4)]
            wA = {"wlin": wslot[0], "wconv": wslot[1],
                  "wa": wslot[2], "wx": wslot[3]}
            wB = {"wlin2": wslot[0], "w1a": wslot[1],
                  "w2a": wslot[2], "w3a": wslot[3]}
            for nm in ["wlin", "wconv", "wa", "wx"]:
                dma(GP, wA[nm][:], wr_in[nm], [], dgp)
            wA_done = {nm: 16 * (i + 1) for i, nm in
                       enumerate(["wlin", "wconv", "wa", "wx"])}

            pha = wsc.enter_context(ExitStack())
            x0t = sb("x0t", [128, KC, TT], F32, pha)
            sqA = sb("sqA", [128, KC, TT], BF16, pha)
            lin_bf = sb("lin_bf", [128, KC, TT], BF16, pha)
            ln1_t = sb("ln1_t", [128, KC, TT], F32, pha)
            a_t = sb("a_t", [128, KC, TT + 1], F32, pha)
            xn_t = sb("xn_t", [128, KC, TT], BF16, pha)
            la_t = sb("la_t", [128, KC, TT], BF16, pha)
            xgel_t = sb("xgel_t", [128, KC, TT], BF16, pha)
            xrg_bf = sb("xrg_bf", [128, KC, TT], BF16, pha)
            it_bf = sb("it_bf", [128, KC, TT], BF16, pha)
            gated_t = sb("gated_t", [128, KC, TT + 1], F32, pha)
            tmpc = sb("tmpcA", [128, TT], F32, pha)
            sgc = sb("sgc", [128, TT], F32, pha)
            rt_c = [sb(f"rt{i}", [128, TT], F32, pha) for i in range(2)]
            h_c = sb("h_cA", [128, TT + 1], F32, pha)
            tmpc2 = h_c[:, 0:TT]

            def f_col0(v):
                for c in range(KC):
                    v.memset(a_t[:, c, 0:1], 0.0)
                    v.memset(gated_t[:, c, 0:1], 0.0)
                return v.engine_nop()
            emit(V, [], f_col0, sv)

            def mm8(wsb, rhs, evbase_mark, per, sem):
                def f(t):
                    for odc in range(KC):
                        if odc >= 2:
                            t.wait_ge(sem.h, marks[evbase_mark] +
                                      per * (odc - 1))
                        last = None
                        for kc in range(KC):
                            last = t.matmul(
                                psM[odc % 2][:],
                                wsb[:, kc, odc * 128:(odc + 1) * 128],
                                rhs[:, kc, :],
                                start=(kc == 0), stop=(kc == KC - 1))
                        last.then_inc(st_.h, 1)
                        st_.n += 1
                    return None
                return f

            def frontA(j):
                xsl = slice(j * TT, (j + 1) * TT)
                w = [] if j == 0 else [(sv, marks[("ln1", j - 1)])]
                dma(SY, x0t[:], xr_in[:, :, xsl], w, dsy)
                marks[("ldx", j)] = dsy.n

                w = [(dsy, marks[("ldx", j)]), (ss, setup_s_done)]
                if j > 0:
                    w.append((st_, marks[("statsA", j - 1)]))
                emit(S, w, lambda s: s.activation(sqA[:], x0t[:], AF.Square),
                     ss)
                marks[("sqA", j)] = ss.n

                def f_stats(t):
                    for c in range(KC):
                        t.matmul(psS1[0:1, :], ones_f32[:, 0:1], x0t[:, c, :],
                                 start=(c == 0), stop=(c == KC - 1))
                    last = None
                    for c in range(KC):
                        last = t.matmul(psS2[0:1, :], ones_bf[:, 0:1],
                                        sqA[:, c, :], start=(c == 0),
                                        stop=(c == KC - 1))
                    return last
                w = [(dsy, marks[("ldx", j)]), (ss, marks[("sqA", j)]),
                     (sv, consts_ready)]
                if j > 0:
                    w.append((ss, marks[("rowsS", j - 1)]))
                emit(T, w, f_stats, st_)
                marks[("statsA", j)] = st_.n

                def f_rows1(s):
                    s.activation(row(0), psS1[0:1, :], AF.Copy,
                                 scale=1.0 / 1024.0)
                    return s.activation(row(1), psS2[0:1, :], AF.Copy,
                                        scale=1.0 / 1024.0)
                w = [(st_, marks[("statsA", j)])]
                if j > 0:
                    w.append((st_, marks[("bcastA", j - 1)]))
                emit(S, w, f_rows1, ss)
                marks[("rowsS", j)] = ss.n

                def f_rows2(v):
                    v.tensor_mul(row(2), row(0), row(0))
                    v.tensor_sub(row(3), row(1), row(2))
                    v.tensor_scalar_max(row(2), row(3), 1e-38)
                    return v.tensor_scalar_add(row(1), row(3), EPS)
                emit(V, [(ss, marks[("rowsS", j)])], f_rows2, sv)

                def f_rows3(s):
                    s.activation(row(4), row(2), AF.Ln)
                    s.activation(row(5), row(1), AF.Ln)
                    return s.activation(row(1), row(5), AF.Exp, scale=-0.5)
                emit(S, [(sv, sv.n)], f_rows3, ss)

                def f_rows4(v):
                    v.tensor_mul(row(2), row(0), row(1))
                    v.tensor_sub(row(3), row(5), row(4))
                    v.tensor_scalar(row(4), row(3), 0.5, -LN1024_HALF,
                                    ALU.mult, ALU.add)
                    return v.tensor_scalar_min(row(3), row(4), 27.631021)
                emit(V, [(ss, ss.n)], f_rows4, sv)

                emit(S, [(sv, sv.n)],
                     lambda s: s.activation(row(4), row(3), AF.Exp,
                                            bias=LN32), ss)
                marks[("rowsS3", j)] = ss.n

                def f_bcast(t):
                    t.matmul(psB[:, 0:TT], ones_f[0:1, :], row(1),
                             start=True, stop=True)
                    t.matmul(psB[:, TT:2 * TT], ones_f[0:1, :], row(2),
                             start=True, stop=True)
                    return t.matmul(psB[:, 2 * TT:3 * TT], ones_f[0:1, :],
                                    row(4), start=True, stop=True)
                w = [(ss, marks[("rowsS3", j)])]
                if j > 0:
                    w.append((sv, marks[("ln1", j - 1)]))
                emit(T, w, f_bcast, st_)
                marks[("bcastA", j)] = st_.n

                def f_ln1(v):
                    for c in range(KC):
                        v.tensor_mul(tmpc[:], x0t[:, c, :], psB[:, 0:TT])
                        v.tensor_sub(ln1_t[:, c, :], tmpc[:],
                                     psB[:, TT:2 * TT])
                        v.scalar_tensor_tensor(xn_t[:, c, :],
                                               ln1_t[:, c, :],
                                               vec["g1"][:, c:c + 1],
                                               psB[:, 2 * TT:3 * TT],
                                               ALU.mult, ALU.mult)
                    return v.engine_nop()
                w = [(st_, marks[("bcastA", j)])]
                if j > 0:
                    w += [(dgp, marks[("dma_ln1", j - 1)]),
                          (st_, marks[("linmm", j - 1)])]
                emit(V, w, f_ln1, sv)
                marks[("ln1", j)] = sv.n

                dma(GP, sr_ln1[:, :, xsl], ln1_t[:],
                    [(sv, marks[("ln1", j)])], dgp)
                marks[("dma_ln1", j)] = dgp.n

            def midA(j):
                xsl = slice(j * TT, (j + 1) * TT)
                marks[("linev_base", j)] = ss.n
                w = [(sv, marks[("ln1", j)]), (dgp, wA_done["wlin"])]
                if j > 0:
                    w.append((ss, marks[("it", j - 1)]))
                emit(T, w, mm8(wA["wlin"], xn_t, ("linev_base", j), 2, ss),
                     None)
                marks[("linmm", j)] = st_.n

                def f_linev(s):
                    base = marks[("linmm", j)] - KC
                    for odc in range(KC):
                        s.wait_ge(st_.h, base + odc + 1)
                        s.activation(lin_bf[:, odc, :], psM[odc % 2][:],
                                     AF.Identity,
                                     bias=vec["blin"][:, odc:odc + 1])
                        last = s.activation(xgel_t[:, odc, :],
                                            psM[odc % 2][:], AF.Gelu,
                                            bias=vec["blin"][:, odc:odc + 1])
                        last.then_inc(ss.h, 2)
                        ss.n += 2
                    return None
                w = []
                if j > 0:
                    w += [(st_, marks[("xrgmm", j - 1)]),
                          (dgp, marks[("dma_xg", j - 1)])]
                emit(S, w, f_linev, None)
                marks[("xgel", j)] = ss.n

                dma(GP, sr_xg[:, :, xsl], xgel_t[:],
                    [(ss, marks[("xgel", j)])], dgp)
                marks[("dma_xg", j)] = dgp.n

                marks[("xrgev_base", j)] = ss.n
                emit(T, [(ss, marks[("xgel", j)]), (dgp, wA_done["wconv"])],
                     mm8(wA["wconv"], lin_bf, ("xrgev_base", j), 1, ss),
                     None)
                marks[("xrgmm", j)] = st_.n

                def f_xrgev(s):
                    base = marks[("xrgmm", j)] - KC
                    for odc in range(KC):
                        s.wait_ge(st_.h, base + odc + 1)
                        last = s.activation(xrg_bf[:, odc, :],
                                            psM[odc % 2][:], AF.Identity,
                                            bias=vec["bconv"][:, odc:odc + 1])
                        last.then_inc(ss.h, 1)
                        ss.n += 1
                    return None
                w = []
                if j > 0:
                    w.append((st_, marks[("itmm", j - 1)]))
                emit(S, w, f_xrgev, None)
                marks[("xrg", j)] = ss.n

                marks[("rtev_base", j)] = ss.n
                emit(T, [(ss, marks[("xrg", j)]), (dgp, wA_done["wa"])],
                     mm8(wA["wa"], xrg_bf, ("rtev_base", j), 1, ss), None)
                marks[("rtmm", j)] = st_.n

                marks[("la_base", j)] = sv.n

                def f_rtev(s):
                    base = marks[("rtmm", j)] - KC
                    for odc in range(KC):
                        s.wait_ge(st_.h, base + odc + 1)
                        if odc >= 2:
                            s.wait_ge(sv.h, marks[("la_base", j)] + odc - 1)
                        last = s.activation(rt_c[odc % 2][:], psM[odc % 2][:],
                                            AF.Sigmoid,
                                            bias=vec["ba"][:, odc:odc + 1])
                        last.then_inc(ss.h, 1)
                        ss.n += 1
                    return None
                w = []
                if j > 0:
                    w.append((dgp, marks[("dma_la", j - 1)]))
                emit(S, w, f_rtev, None)
                marks[("rt", j)] = ss.n

                def f_la(v):
                    base = marks[("rt", j)] - KC
                    for odc in range(KC):
                        v.wait_ge(ss.h, base + odc + 1)
                        last = v.tensor_scalar(la_t[:, odc, :],
                                               rt_c[odc % 2][:],
                                               logaC[:, odc:odc + 1], None,
                                               ALU.mult)
                        last.then_inc(sv.h, 1)
                        sv.n += 1
                    return None
                emit(V, [], f_la, None)
                marks[("la", j)] = sv.n

                dma(GP, sr_la[:, :, xsl], la_t[:],
                    [(sv, marks[("la", j)])], dgp)
                marks[("dma_la", j)] = dgp.n

                marks[("itev_base", j)] = ss.n
                emit(T, [(ss, marks[("rt", j)]), (dgp, wA_done["wx"])],
                     mm8(wA["wx"], xrg_bf, ("itev_base", j), 1, ss), None)
                marks[("itmm", j)] = st_.n

                def f_itev(s):
                    base = marks[("itmm", j)] - KC
                    for odc in range(KC):
                        s.wait_ge(st_.h, base + odc + 1)
                        last = s.activation(it_bf[:, odc, :], psM[odc % 2][:],
                                            AF.Sigmoid,
                                            bias=vec["bx"][:, odc:odc + 1])
                        last.then_inc(ss.h, 1)
                        ss.n += 1
                    return None
                w = []
                if j > 0:
                    w.append((sv, marks[("gated", j - 1)]))
                emit(S, w, f_itev, None)
                marks[("it", j)] = ss.n

            def tailA(j):
                xsl = slice(j * TT, (j + 1) * TT)
                w = [(sv, marks[("la", j)]), (dgp, marks[("dma_la", j)])]
                if j > 0:
                    w.append((sv, marks[("gated", j - 1)]))
                emit(S, w,
                     lambda s: s.activation(a_t[:, :, 1:TT + 1], la_t[:],
                                            AF.Exp), ss)
                marks[("a", j)] = ss.n

                def f_om(v):
                    for c in range(KC):
                        v.tensor_mul(tmpc[:], a_t[:, c, 1:TT + 1],
                                     a_t[:, c, 1:TT + 1])
                        v.tensor_scalar(tmpc2, tmpc[:], -1.0, 1.0,
                                        ALU.mult, ALU.add)
                        v.tensor_scalar_max(gated_t[:, c, 1:TT + 1], tmpc2,
                                            1e-38)
                    return v.engine_nop()
                w = [(ss, marks[("a", j)])]
                if j > 0:
                    w.append((dgp, marks[("dma_g", j - 1)]))
                emit(V, w, f_om, sv)
                marks[("om", j)] = sv.n

                def f_sg(s):
                    for c in range(KC):
                        s.activation(sgc[:], gated_t[:, c, 1:TT + 1], AF.Ln)
                        last = s.activation(gated_t[:, c, 1:TT + 1], sgc[:],
                                            AF.Exp, scale=0.5)
                        last.then_inc(ss.h, 1)
                        ss.n += 1
                    return None
                emit(S, [(sv, marks[("om", j)])], f_sg, None)
                marks[("sg", j)] = ss.n

                def f_gsc(v):
                    for c in range(KC):
                        v.wait_ge(ss.h, marks[("sg", j)] - KC + c + 1)
                        v.tensor_mul(tmpc[:], gated_t[:, c, 1:TT + 1],
                                     it_bf[:, c, :])
                        v.tensor_mul(gated_t[:, c, 1:TT + 1], tmpc[:],
                                     xrg_bf[:, c, :])
                        v.tensor_tensor_scan(h_c[:], a_t[:, c, 0:TT + 1],
                                             gated_t[:, c, 0:TT + 1], 0.0,
                                             ALU.mult, ALU.add) \
                            .then_inc(sv.h, 1)
                        sv.n += 1
                        v.wait_ge(sv.h, sv.n)
                        v.tensor_copy(gated_t[:, c, 0:1], h_c[:, TT:TT + 1])
                        if j == NT - 1:
                            v.tensor_copy(carry[:, c:c + 1],
                                          h_c[:, TT:TT + 1])
                    return v.engine_nop()
                emit(V, [(ss, marks[("it", j)])], f_gsc, sv)
                marks[("gated", j)] = sv.n

                dma(GP, sr_g[:, :, xsl], gated_t[:, :, 1:TT + 1],
                    [(sv, marks[("gated", j)])], dgp)
                marks[("dma_g", j)] = dgp.n

            frontA(0)
            for j in range(NT):
                midA(j)
                if j + 1 < NT:
                    frontA(j + 1)
                tailA(j)

            dma(SY, cc_in, carry[:], [(sv, marks[("gated", NT - 1)])], dsy)
            cc_up = dsy.n

            emit(GP, [(dsy, cc_up)],
                 lambda g: g.collective_compute(
                     "AllGather", ALU.bypass,
                     replica_groups=[[0, 1], [2, 3], [4, 5], [6, 7]],
                     ins=[cc_in], outs=[cc_out]), ccs)

            dma(SY, hraw[:], cc_out[0], [(ccs, 1)], dsy)
            hraw_ld = dsy.n

            for nm, amark in [("wlin2", ("linmm", NT - 1)),
                              ("w1a", ("xrgmm", NT - 1)),
                              ("w2a", ("rtmm", NT - 1)),
                              ("w3a", ("itmm", NT - 1))]:
                emit(GP, [(st_, marks[amark])],
                     lambda g, nm=nm: g.dma_start(out=wB[nm][:],
                                                  in_=wr_in[nm]), dgp, 16)
                marks[("wB", nm)] = dgp.n

            a_end_st, a_end_ss, a_end_sv = st_.n, ss.n, sv.n
            a_end_dgp = dgp.n
            pha.close()

            phb = wsc.enter_context(ExitStack())
            laB = sb("laB", [128, KC, TT], BF16, phb)
            gatB = sb("gatB", [128, KC, TT + 1], F32, phb)
            xgB = sb("xgB", [128, KC, TT], BF16, phb)
            aB = sb("aB", [128, KC, TT + 1], F32, phb)
            outsB = sb("outsB", [128, KC, TT], BF16, phb)
            gateB = sb("gateB", [128, KC, TT], BF16, phb)
            xghB = sb("xghB", [128, KC, TT], BF16, phb)
            res2 = sb("res2", [128, KC, TT], F32, phb)
            sqB = sb("sqB", [128, TT], BF16, phb)
            xmB = sb("xmB", [128, KC, TT], BF16, phb)
            combB = sb("combB", [128, KC, TT], BF16, phb)
            hB = sb("hB", [128, TT + 1], F32, phb)
            tvB1 = sb("tvB", [128, TT], F32, phb)
            tvB = [tvB1, tvB1]
            x1B = [sb(f"x1B{i}", [128, TT], F32, phb) for i in range(2)]
            ln1B = sb("ln1B", [128, KC, TT], F32, phb)
            x0B = ln1B

            aend = [(sv, a_end_sv), (ss, a_end_ss), (st_, a_end_st)]

            def f_bcol0(v):
                for c in range(KC):
                    v.memset(aB[:, c, 0:1], 0.0)
                    v.tensor_mul(gatB[:, c, 0:1], hraw[:, c:c + 1],
                                 vec["mask"][:, c:c + 1])
                return v.engine_nop()
            emit(V, [(dsy, hraw_ld)] + aend, f_bcol0, sv)
            bcol0_ready = sv.n

            def frontB(j):
                xsl = slice(j * TT, (j + 1) * TT)
                w = [(dgp, marks[("dma_la", j)])]
                w += aend if j == 0 else [(ss, marks[("aBexp", j - 1)])]
                dma(SY, laB[:], sr_la[:, :, xsl], w, dsy)
                marks[("ld_laB", j)] = dsy.n
                w = [(dgp, marks[("dma_g", j)])]
                w += aend if j == 0 else [(sv, marks[("scanB", j - 1)])]
                dma(SY, gatB[:, :, 1:TT + 1], sr_g[:, :, xsl], w, dsy)
                marks[("ld_gB", j)] = dsy.n
                if j == 0:
                    dma(SY, ln1B[:], sr_ln1[:, :, xsl],
                        [(dgp, marks[("dma_ln1", j)])] + aend, dsy)
                    marks[("ld_ln1B", j)] = dsy.n
                w = [(dgp, marks[("dma_xg", j)])]
                w += aend if j == 0 else [(sv, marks[("scanB", j - 1)])]
                dma(SY, xgB[:], sr_xg[:, :, xsl], w, dsy)
                marks[("ld_xgB", j)] = dsy.n

                w = [(dsy, marks[("ld_laB", j)])]
                if j == 0:
                    w += [(st_, a_end_st)]
                else:
                    w.append((sv, marks[("scanB", j - 1)]))
                emit(S, w, lambda s: s.activation(aB[:, :, 1:TT + 1],
                                                  laB[:], AF.Exp), ss)
                marks[("aBexp", j)] = ss.n

                def f_scan(v):
                    for c in range(KC):
                        v.tensor_tensor_scan(hB[:], aB[:, c, 0:TT + 1],
                                             gatB[:, c, 0:TT + 1], 0.0,
                                             ALU.mult, ALU.add) \
                            .then_inc(sv.h, 1)
                        sv.n += 1
                        v.wait_ge(sv.h, sv.n)
                        v.tensor_mul(xghB[:, c, :], hB[:, 1:TT + 1],
                                     xgB[:, c, :])
                        v.tensor_copy(gatB[:, c, 0:1], hB[:, TT:TT + 1])
                    return v.engine_nop()
                w = [(ss, marks[("aBexp", j)]), (dsy, marks[("ld_gB", j)]),
                     (dsy, marks[("ld_xgB", j)])]
                if j == 0:
                    w.append((sv, bcol0_ready))
                else:
                    w.append((st_, marks[("xrmm", j - 1)]))
                emit(V, w, f_scan, sv)
                marks[("scanB", j)] = sv.n

                if DEBUG_TAPS:
                    dma(SY, dbg["d_xgh"][:, :, xsl], xghB[:],
                        [(sv, marks[("scanB", j)])], dsy)

            def mainB(j):
                xsl = slice(j * TT, (j + 1) * TT)
                marks[("res2_base", j)] = sv.n
                w = [(sv, marks[("scanB", j)]), (dgp, marks[("wB", "wlin2")]),
                     (dsy, marks[("ld_ln1B", j)])]
                if j > 0:
                    w.append((sv, marks[("x1", j - 1)]))
                emit(T, w, mm8(wB["wlin2"], xghB, ("res2_base", j), 1, sv),
                     None)
                marks[("xrmm", j)] = st_.n

                def f_res2(v):
                    base = marks[("xrmm", j)] - KC
                    for odc in range(KC):
                        v.wait_ge(st_.h, base + odc + 1)
                        last = v.scalar_tensor_tensor(
                            res2[:, odc, :], psM[odc % 2][:],
                            vec["blin2"][:, odc:odc + 1], ln1B[:, odc, :],
                            ALU.add, ALU.add)
                        last.then_inc(sv.h, 1)
                        sv.n += 1
                    return None
                emit(V, [(dsy, marks[("ld_ln1B", j)])], f_res2, None)
                marks[("res2", j)] = sv.n
                if DEBUG_TAPS:
                    dma(SY, dbg["d_res2"][:, :, xsl], res2[:],
                        [(sv, marks[("res2", j)])], dsy)

                marks[("sqB_base", j)] = ss.n
                marks[("statsB_base", j)] = st_.n

                def f_sqB(s):
                    base = marks[("res2", j)] - KC
                    for c in range(KC):
                        s.wait_ge(sv.h, base + c + 1)
                        if c >= 1:
                            s.wait_ge(st_.h, marks[("statsB_base", j)] + c)
                        last = s.activation(sqB[:], res2[:, c, :], AF.Square)
                        last.then_inc(ss.h, 1)
                        ss.n += 1
                    return None
                emit(S, [], f_sqB, None)

                def f_statsB(t):
                    base = marks[("sqB_base", j)]
                    for c in range(KC):
                        t.wait_ge(ss.h, base + c + 1)
                        last = t.matmul(psS2[0:1, :], ones_bf[:, 0:1],
                                        sqB[:], start=(c == 0),
                                        stop=(c == KC - 1))
                        last.then_inc(st_.h, 1)
                        st_.n += 1
                    return None
                w = []
                if j > 0:
                    w.append((ss, marks[("rowsB1", j - 1)]))
                emit(T, w, f_statsB, None)
                marks[("statsB", j)] = st_.n

                emit(S, [(st_, marks[("statsB", j)])],
                     lambda s: s.activation(row(0), psS2[0:1, :], AF.Copy),
                     ss)
                marks[("rowsB1", j)] = ss.n

                emit(V, [(ss, marks[("rowsB1", j)])],
                     lambda v: v.tensor_scalar_max(row(1), row(0), 1e-24), sv)
                marks[("rowsB2", j)] = sv.n

                def f_rowsB3(s):
                    s.activation(row(2), row(1), AF.Ln)
                    return s.activation(row(3), row(2), AF.Exp, scale=-0.5,
                                        bias=LN32)
                emit(S, [(sv, marks[("rowsB2", j)])], f_rowsB3, ss)
                marks[("rowsB3", j)] = ss.n

                w = [(ss, marks[("rowsB3", j)])]
                w.append((sv, a_end_sv) if j == 0
                         else (sv, marks[("xm", j - 1)]))
                emit(T, w,
                     lambda t: t.matmul(psB[:, 0:TT], ones_f[0:1, :], row(3),
                                        start=True, stop=True), st_)
                marks[("bcastB", j)] = st_.n

                def f_xm(v):
                    for c in range(KC):
                        v.scalar_tensor_tensor(xmB[:, c, :], res2[:, c, :],
                                               vec["g2"][:, c:c + 1],
                                               psB[:, 0:TT],
                                               ALU.mult, ALU.mult)
                    return v.engine_nop()
                w = [(st_, marks[("bcastB", j)])]
                if j > 0:
                    w.append((st_, marks[("w2mm", j - 1)]))
                emit(V, w, f_xm, sv)
                marks[("xm", j)] = sv.n
                if DEBUG_TAPS:
                    dma(SY, dbg["d_xm"][:, :, xsl], xmB[:],
                        [(sv, marks[("xm", j)])], dsy)

                marks[("outs_base", j)] = ss.n
                emit(T, [(sv, marks[("xm", j)]), (dgp, marks[("wB", "w1a")])],
                     mm8(wB["w1a"], xmB, ("outs_base", j), 1, ss), None)
                marks[("w1mm", j)] = st_.n

                def f_outs(s):
                    base = marks[("w1mm", j)] - KC
                    for odc in range(KC):
                        s.wait_ge(st_.h, base + odc + 1)
                        last = s.activation(outsB[:, odc, :],
                                            psM[odc % 2][:], AF.Sigmoid,
                                            bias=vec["b1"][:, odc:odc + 1])
                        last.then_inc(ss.h, 1)
                        ss.n += 1
                    return None
                w = []
                if j > 0:
                    w.append((ss, marks[("gate", j - 1)]))
                emit(S, w, f_outs, None)
                marks[("outs", j)] = ss.n

                w = [(ss, marks[("outs", j)])]
                if j > 0:
                    w.append((sv, marks[("comb", j - 1)]))
                emit(S, w,
                     lambda s: s.activation(gateB[:], outsB[:], AF.Gelu), ss)
                marks[("gate", j)] = ss.n

            def tailB(j):
                xsl = slice(j * TT, (j + 1) * TT)
                marks[("comb_base", j)] = sv.n
                emit(T, [(ss, marks[("gate", j)]),
                         (dgp, marks[("wB", "w2a")])],
                     mm8(wB["w2a"], xmB, ("comb_base", j), 1, sv), None)
                marks[("w2mm", j)] = st_.n

                def f_comb(v):
                    base = marks[("w2mm", j)] - KC
                    for odc in range(KC):
                        v.wait_ge(st_.h, base + odc + 1)
                        last = v.scalar_tensor_tensor(
                            combB[:, odc, :], psM[odc % 2][:],
                            vec["b2"][:, odc:odc + 1], gateB[:, odc, :],
                            ALU.add, ALU.mult)
                        last.then_inc(sv.h, 1)
                        sv.n += 1
                    return None
                emit(V, [], f_comb, None)
                marks[("comb", j)] = sv.n
                if DEBUG_TAPS:
                    dma(SY, dbg["d_comb"][:, :, xsl], combB[:],
                        [(sv, marks[("comb", j)])], dsy)

                dma(SY, x0B[:], xr_in[:, :, xsl], [(sv, marks[("res2", j)])],
                    dsy)
                marks[("ld_x0B", j)] = dsy.n

                marks[("x1_base", j)] = sv.n
                emit(T, [(sv, marks[("comb", j)]),
                         (dgp, marks[("wB", "w3a")])],
                     mm8(wB["w3a"], combB, ("x1_base", j), 2, sv), None)
                marks[("w3mm", j)] = st_.n

                marks[("x1dma_base", j)] = dgp.n

                def f_x1(v):
                    base = marks[("w3mm", j)] - KC
                    for odc in range(KC):
                        v.wait_ge(st_.h, base + odc + 1)
                        if odc >= 2:
                            v.wait_ge(dgp.h, marks[("x1dma_base", j)] +
                                      16 * (odc - 1))
                        v.scalar_tensor_tensor(tvB[odc % 2][:],
                                               psM[odc % 2][:],
                                               vec["b3"][:, odc:odc + 1],
                                               res2[:, odc, :],
                                               ALU.add, ALU.add)
                        last = v.tensor_add(x1B[odc % 2][:], tvB[odc % 2][:],
                                            x0B[:, odc, :])
                        last.then_inc(sv.h, 2)
                        sv.n += 2
                    return None
                emit(V, [(dsy, marks[("ld_x0B", j)])], f_x1, None)
                marks[("x1", j)] = sv.n

                def f_x1dma(g):
                    base = marks[("x1", j)] - 2 * KC
                    for odc in range(KC):
                        g.wait_ge(sv.h, base + 2 * (odc + 1))
                        last = g.dma_start(out=sr_x1[:, odc, xsl],
                                           in_=x1B[odc % 2][:])
                        last.then_inc(dgp.h, 16)
                        dgp.n += 16
                    return None
                emit(GP, [], f_x1dma, None)
                marks[("x1dma", j)] = dgp.n

                if j + 1 < NT:
                    nsl = slice((j + 1) * TT, (j + 2) * TT)
                    dma(SY, ln1B[:], sr_ln1[:, :, nsl],
                        [(dgp, marks[("dma_ln1", j + 1)]),
                         (sv, marks[("x1", j)])], dsy)
                    marks[("ld_ln1B", j + 1)] = dsy.n

            frontB(0)
            for j in range(NT):
                mainB(j)
                if j + 1 < NT:
                    frontB(j + 1)
                tailB(j)

            b_end_st, b_end_ss, b_end_sv = st_.n, ss.n, sv.n
            b_end_dgp, b_end_dsy = dgp.n, dsy.n
            phb.close()

        with ExitStack() as phc:
            w1sb = sb("w1sb", [128, KC, H], BF16, phc)
            w2sb = sb("w2sb", [128, HC, D], FP8, phc)
            first = True
            for q in range(4):
                w = [(st_, b_end_st), (sv, b_end_sv), (ss, b_end_ss),
                     (dsy, b_end_dsy)] if first else []
                first = False
                emit(GP, w,
                     lambda g, q=q: g.dma_start(
                         out=w1sb[:, :, q * 1024:(q + 1) * 1024],
                         in_=fw1_r[:, :, q * 1024:(q + 1) * 1024]),
                     dgp, 16)
                marks[("w1q", q)] = dgp.n
            emit(GP, [], lambda g: g.dma_start(out=w2sb[:], in_=fw2_r),
                 dgp, 16)
            w2_done = dgp.n

            x1C = [sb(f"x1C{i}", [128, KC, TT], F32, phc) for i in range(2)]
            sqC = sb("sqC", [128, KC, TT], BF16, phc)
            ln2C = [sb(f"ln2C{i}", [128, KC, TT], BF16, phc)
                    for i in range(2)]
            hqC = sb("hqC", [128, HC, TT], FP8, phc)
            o16 = [sb(f"o16_{i}", [128, TT], FP16, phc) for i in range(2)]
            xoC = [sb(f"xoC{i}", [128, TT], F32, phc) for i in range(2)]
            tmpC = sb("tmpC", [128, TT], F32, phc)

            bend = [(sv, b_end_sv), (ss, b_end_ss), (st_, b_end_st)]

            def frontC(j):
                xsl = slice(j * TT, (j + 1) * TT)
                x1t = x1C[j % 2]
                w = [(dgp, marks[("x1dma", j)])]
                if j == 0:
                    w += bend
                if j > 1:
                    w.append((sv, marks[("xoutV", j - 2)]))
                dma(SY, x1t[:], sr_x1[:, :, xsl], w, dsy)
                marks[("ld_x1C", j)] = dsy.n

                w = [(dsy, marks[("ld_x1C", j)])]
                if j > 0:
                    w.append((st_, marks[("statsC", j - 1)]))
                emit(S, w, lambda s: s.activation(sqC[:], x1t[:],
                                                  AF.Square), ss)
                marks[("sqC", j)] = ss.n

                def f_statsC(t):
                    for c in range(KC):
                        t.matmul(psS1[0:1, :], ones_f32[:, 0:1],
                                 x1t[:, c, :], start=(c == 0),
                                 stop=(c == KC - 1))
                    last = None
                    for c in range(KC):
                        last = t.matmul(psS2[0:1, :], ones_bf[:, 0:1],
                                        sqC[:, c, :], start=(c == 0),
                                        stop=(c == KC - 1))
                    return last
                w = [(dsy, marks[("ld_x1C", j)]), (ss, marks[("sqC", j)])]
                if j > 0:
                    w.append((ss, marks[("rowsC1", j - 1)]))
                emit(T, w, f_statsC, st_)
                marks[("statsC", j)] = st_.n

                def f_rowsC1(s):
                    s.activation(row(0), psS1[0:1, :], AF.Copy,
                                 scale=1.0 / 1024.0)
                    return s.activation(row(1), psS2[0:1, :], AF.Copy,
                                        scale=1.0 / 1024.0)
                emit(S, [(st_, marks[("statsC", j)])], f_rowsC1, ss)
                marks[("rowsC1", j)] = ss.n

                def f_rowsC2(v):
                    v.tensor_mul(row(2), row(0), row(0))
                    v.tensor_sub(row(3), row(1), row(2))
                    return v.tensor_scalar_add(row(2), row(3), EPS)
                emit(V, [(ss, marks[("rowsC1", j)])], f_rowsC2, sv)

                def f_rowsC3(s):
                    s.activation(row(4), row(2), AF.Ln)
                    return s.activation(row(5), row(4), AF.Exp, scale=-0.5)
                emit(S, [(sv, sv.n)], f_rowsC3, ss)

                emit(V, [(ss, ss.n)],
                     lambda v: v.tensor_mul(row(1), row(0), row(5)), sv)
                marks[("rowsC4", j)] = sv.n

                def f_bcastC(t):
                    t.matmul(psB[:, 0:TT], ones_f[0:1, :], row(5),
                             start=True, stop=True)
                    return t.matmul(psB[:, TT:2 * TT], ones_f[0:1, :],
                                    row(1), start=True, stop=True)
                w = [(sv, marks[("rowsC4", j)])]
                if j > 0:
                    w.append((sv, marks[("ln2", j - 1)]))
                emit(T, w, f_bcastC, st_)
                marks[("bcastC", j)] = st_.n

                def f_ln2(v):
                    for c in range(KC):
                        v.tensor_mul(tmpC[:], x1t[:, c, :], psB[:, 0:TT])
                        v.tensor_sub(ln2C[j % 2][:, c, :], tmpC[:],
                                     psB[:, TT:2 * TT])
                    return v.engine_nop()
                w = [(st_, marks[("bcastC", j)])]
                if j > 1:
                    w.append((st_, marks[("w1mmC", j - 2)]))
                emit(V, w, f_ln2, sv)
                marks[("ln2", j)] = sv.n

            def mainC(j):
                xsl = slice(j * TT, (j + 1) * TT)
                x1t = x1C[j % 2]
                marks[("hq_base", j)] = ss.n

                def f_w1(t):
                    for hc in range(HC):
                        if hc >= 2:
                            t.wait_ge(ss.h, marks[("hq_base", j)] + hc - 1)
                        if hc == 8:
                            t.wait_ge(dgp.h, marks[("w1q", 1)])
                        if hc == 16:
                            t.wait_ge(dgp.h, marks[("w1q", 2)])
                        if hc == 24:
                            t.wait_ge(dgp.h, marks[("w1q", 3)])
                        last = None
                        for kc in range(KC):
                            last = t.matmul(
                                psM[hc % 2][:],
                                w1sb[:, kc, hc * 128:(hc + 1) * 128],
                                ln2C[j % 2][:, kc, :],
                                start=(kc == 0), stop=(kc == KC - 1))
                        last.then_inc(st_.h, 1)
                        st_.n += 1
                    return None
                w = [(sv, marks[("ln2", j)]), (dgp, marks[("w1q", 0)])]
                if j > 0:
                    w.append((ss, marks[("o16", j - 1)]))
                emit(T, w, f_w1, None)
                marks[("w1mmC", j)] = st_.n

                def f_hq(s):
                    base = marks[("w1mmC", j)] - HC
                    for hc in range(HC):
                        s.wait_ge(st_.h, base + hc + 1)
                        last = s.activation(hqC[:, hc, :], psM[hc % 2][:],
                                            AF.Relu,
                                            bias=vec["fb1"][:, hc:hc + 1])
                        last.then_inc(ss.h, 1)
                        ss.n += 1
                    return None
                w = []
                if j > 0:
                    w.append((st_, marks[("w2mmC", j - 1)]))
                emit(S, w, f_hq, None)
                marks[("hq", j)] = ss.n

            def tailC(j):
                xsl = slice(j * TT, (j + 1) * TT)
                x1t = x1C[j % 2]
                marks[("o16_base", j)] = ss.n

                def f_w2(t):
                    for odc in range(KC):
                        if odc >= 2:
                            t.wait_ge(ss.h, marks[("o16_base", j)] + odc - 1)
                        last = None
                        for hc in range(HC):
                            last = t.matmul(
                                psM[odc % 2][:],
                                w2sb[:, hc, odc * 128:(odc + 1) * 128],
                                hqC[:, hc, :],
                                start=(hc == 0), stop=(hc == HC - 1))
                        last.then_inc(st_.h, 1)
                        st_.n += 1
                    return None
                emit(T, [(ss, marks[("hq", j)]), (dgp, w2_done)], f_w2, None)
                marks[("w2mmC", j)] = st_.n

                marks[("xoutV_base", j)] = sv.n

                def f_o16(s):
                    base = marks[("w2mmC", j)] - KC
                    for odc in range(KC):
                        s.wait_ge(st_.h, base + odc + 1)
                        if odc >= 2:
                            s.wait_ge(sv.h, marks[("xoutV_base", j)] +
                                      odc - 1)
                        last = s.activation(o16[odc % 2][:], psM[odc % 2][:],
                                            AF.Identity,
                                            bias=bq32[:, odc:odc + 1])
                        last.then_inc(ss.h, 1)
                        ss.n += 1
                    return None
                emit(S, [], f_o16, None)
                marks[("o16", j)] = ss.n

                marks[("xodma_base", j)] = dsy.n

                def f_xout(v):
                    base = marks[("o16", j)] - KC
                    for odc in range(KC):
                        v.wait_ge(ss.h, base + odc + 1)
                        if odc >= 2:
                            v.wait_ge(dsy.h, marks[("xodma_base", j)] +
                                      16 * (odc - 1))
                        last = v.tensor_add(xoC[odc % 2][:], o16[odc % 2][:],
                                            x1t[:, odc, :])
                        last.then_inc(sv.h, 1)
                        sv.n += 1
                    return None
                emit(V, [], f_xout, None)
                marks[("xoutV", j)] = sv.n

                def f_xodma(sync):
                    base = marks[("xoutV", j)] - KC
                    for odc in range(KC):
                        sync.wait_ge(sv.h, base + odc + 1)
                        last = sync.dma_start(out=xr_out[:, odc, xsl],
                                              in_=xoC[odc % 2][:])
                        last.then_inc(dsy.h, 16)
                        dsy.n += 16
                    return None
                emit(SY, [], f_xodma, None)
                marks[("xodma", j)] = dsy.n

            frontC(0)
            for j in range(NT):
                mainC(j)
                if j + 1 < NT:
                    frontC(j + 1)
                tailC(j)

            emit(SY, [(dsy, dsy.n)], lambda e: None, None)

    return nc


_CACHE = {}


def _get_prog():
    if "nc" not in _CACHE:
        _CACHE["nc"] = _build()
    return _CACHE["nc"]


def kernel(**inputs):
    inp = {k: np.asarray(v) for k, v in inputs.items()}

    def vstage(v, kc):
        v = np.asarray(v, np.float32)
        return np.ascontiguousarray(v.reshape(kc, 128).T)

    def wt(name):
        return np.ascontiguousarray(np.asarray(inp[name], np.float32).T)

    shared = {
        "wlin": wt("L0_attn_W_lin"), "wconv": wt("L0_attn_W_conv"),
        "wa": wt("L0_attn_Wa"), "wx": wt("L0_attn_Wx"),
        "wlin2": wt("L0_attn_W_lin2"), "w1a": wt("L0_attn_W1"),
        "w2a": wt("L0_attn_W2"), "w3a": wt("L0_attn_W3"),
        "fw1": wt("L0_ffn_w1"), "fw2": wt("L0_ffn_w2"),
        "vg1": vstage(inp["L0_attn_g1"], KC),
        "vg2": vstage(inp["L0_attn_g2"], KC),
        "vblin": vstage(inp["L0_attn_b_lin"], KC),
        "vbconv": vstage(inp["L0_attn_b_conv"], KC),
        "vba": vstage(inp["L0_attn_ba"], KC),
        "vbx": vstage(inp["L0_attn_bx"], KC),
        "vblin2": vstage(inp["L0_attn_b_lin2"], KC),
        "vb1": vstage(inp["L0_attn_b1"], KC),
        "vb2": vstage(inp["L0_attn_b2"], KC),
        "vb3": vstage(inp["L0_attn_b3"], KC),
        "vlam": vstage(inp["L0_attn_Lam"], KC),
        "vfb2": vstage(inp["L0_ffn_b2"], KC),
        "vfb1": vstage(inp["L0_ffn_b1"], HC),
    }

    x = np.asarray(inp["L0_attn_x"], np.float32)
    in_maps = []
    for c in range(NCORE):
        b, half = c // 2, c % 2
        t0 = half * TPC
        m = dict(shared)
        m["x"] = np.ascontiguousarray(x[b, t0:t0 + TPC, :].T)
        m["vmask"] = np.full((128, KC), float(half), np.float32)
        in_maps.append(m)

    nc = _get_prog()
    res = run_bass_kernel_spmd(nc, in_maps, core_ids=list(range(NCORE)))

    out = np.empty((4, 4096, D), np.float32)
    for c in range(NCORE):
        b, half = c // 2, c % 2
        t0 = half * TPC
        out[b, t0:t0 + TPC, :] = res.results[c]["xout"].T
    return out


# version 07
# speedup vs baseline: 1.1832x; 1.0172x over previous
import numpy as np
import concourse.bass as bass
import concourse.mybir as mybir
from concourse.bass_utils import run_bass_kernel_spmd
from contextlib import ExitStack

F32 = mybir.dt.float32
BF16 = mybir.dt.bfloat16
FP16 = mybir.dt.float16
FP8 = mybir.dt.float8e4
AF = mybir.ActivationFunctionType
ALU = mybir.AluOpType

D = 1024
KC = 8
TPC = 2048
NT = 4
TT = 512
H = 4096
HC = 32
NCORE = 8
EPS = 1e-5
LN32 = float(np.log(32.0))
LN1024_HALF = 0.5 * float(np.log(1024.0))

_CONSTS = [0.0, 1.0, -1.0, 0.5, -0.5, -0.125, 1.0 / 1024.0, LN32]

DEBUG_TAPS = False

_VEC_NAMES = ["g1", "g2", "blin", "bconv", "ba", "bx", "blin2", "b1", "b2",
              "b3", "lam", "fb2"]


class _Cnt:
    def __init__(self, handle):
        self.h = handle
        self.n = 0


def _build():
    nc = bass.Bass("TRN2", target_bir_lowering=False, debug=False)

    x_in = nc.dram_tensor("x", [D, TPC], F32, kind="ExternalInput").ap()
    w_in = {}
    for nm in ["wlin", "wconv", "wa", "wx", "wlin2", "w1a", "w2a", "w3a"]:
        w_in[nm] = nc.dram_tensor(nm, [D, D], F32, kind="ExternalInput").ap()
    fw1_in = nc.dram_tensor("fw1", [D, H], F32, kind="ExternalInput").ap()
    fw2_in = nc.dram_tensor("fw2", [H, D], F32, kind="ExternalInput").ap()
    v_in = {}
    for nm in _VEC_NAMES:
        v_in[nm] = nc.dram_tensor("v" + nm, [128, KC], F32,
                                  kind="ExternalInput").ap()
    v_in["fb1"] = nc.dram_tensor("vfb1", [128, HC], F32,
                                 kind="ExternalInput").ap()
    v_in["mask"] = nc.dram_tensor("vmask", [128, KC], F32,
                                  kind="ExternalInput").ap()

    x_out = nc.dram_tensor("xout", [D, TPC], F32, kind="ExternalOutput").ap()

    dbg = {}
    if DEBUG_TAPS:
        for nm, dt_ in [("d_xgh", BF16), ("d_res2", F32), ("d_xm", BF16),
                        ("d_comb", BF16)]:
            dbg[nm] = nc.dram_tensor(nm, [D, TPC], dt_,
                                     kind="ExternalOutput").ap() \
                .rearrange("(c p) t -> p c t", p=128)

    s_ln1 = nc.dram_tensor("s_ln1", [D, TPC], F32, kind="Internal").ap()
    s_la = nc.dram_tensor("s_la", [D, TPC], BF16, kind="Internal").ap()
    s_g = nc.dram_tensor("s_g", [D, TPC], F32, kind="Internal").ap()
    s_xg = nc.dram_tensor("s_xg", [D, TPC], BF16, kind="Internal").ap()
    s_x1 = nc.dram_tensor("s_x1", [D, TPC], F32, kind="Internal").ap()
    cc_in = nc.dram_tensor("cc_in", [128, KC], F32, kind="Internal").ap()
    cc_out = nc.dram_tensor("cc_out", [2, 128, KC], F32, kind="Internal").ap()

    xr_in = x_in.rearrange("(c p) t -> p c t", p=128)
    xr_out = x_out.rearrange("(c p) t -> p c t", p=128)
    wr_in = {nm: a.rearrange("(c p) o -> p c o", p=128)
             for nm, a in w_in.items()}
    fw1_r = fw1_in.rearrange("(c p) o -> p c o", p=128)
    fw2_r = fw2_in.rearrange("(c p) o -> p c o", p=128)
    sr_ln1 = s_ln1.rearrange("(c p) t -> p c t", p=128)
    sr_la = s_la.rearrange("(c p) t -> p c t", p=128)
    sr_g = s_g.rearrange("(c p) t -> p c t", p=128)
    sr_xg = s_xg.rearrange("(c p) t -> p c t", p=128)
    sr_x1 = s_x1.rearrange("(c p) t -> p c t", p=128)

    with ExitStack() as top:
        def sb(name, shape, dt_, stack):
            return stack.enter_context(nc.sbuf_tensor(name, shape, dt_))

        cpool = sb("cpool", [128, len(_CONSTS)], F32, top)
        for i, val in enumerate(_CONSTS):
            nc.const_aps.register(F32, val, cpool[:, i:i + 1])
        ones_bf = sb("ones_bf", [128, 1], BF16, top)
        ones_f32 = sb("ones_f32", [128, 1], F32, top)
        ones_f = sb("ones_f", [1, 128], F32, top)
        vec = {nm: sb("sv_" + nm, [128, KC], F32, top) for nm in _VEC_NAMES}
        vec["fb1"] = sb("sv_fb1", [128, HC], F32, top)
        vec["mask"] = sb("sv_mask", [128, KC], F32, top)
        logaC = sb("logaC", [128, KC], F32, top)
        lam_t = sb("lam_t", [128, KC], F32, top)
        bq16 = sb("bq16", [128, KC], FP16, top)
        bq32 = sb("bq32", [128, KC], F32, top)
        carry = sb("carry", [128, KC], F32, top)
        hraw = sb("hraw", [128, KC], F32, top)
        rows_t = sb("rows_t", [1, 6 * TT], F32, top)

        def row(i):
            return rows_t[0:1, i * TT:(i + 1) * TT]

        psS1 = top.enter_context(nc.psum_tensor("psS1", [128, TT], F32))
        psS2 = top.enter_context(nc.psum_tensor("psS2", [128, TT], F32))
        psB = top.enter_context(nc.psum_tensor("psB", [128, 3 * TT], F32))
        psM0 = top.enter_context(nc.psum_tensor("psM0", [128, TT], F32))
        psM1 = top.enter_context(nc.psum_tensor("psM1", [128, TT], F32))
        psM = [psM0, psM1]

        sems = {nm: _Cnt(top.enter_context(nc.semaphore(nm)))
                for nm in ["dsy", "dgp", "sv", "ss", "st", "cc"]}
        dsy, dgp, sv, ss, st_, ccs = (sems[n] for n in
                                      ["dsy", "dgp", "sv", "ss", "st", "cc"])

        block = top.enter_context(nc.Block())
        SY, GP, V, S, T = (block.sync, block.gpsimd, block.vector,
                           block.scalar, block.tensor)

        def emit(engine_sel, waits, fn, inc=None, inc_k=1):
            def body(eng):
                for cnt, val in waits:
                    if val > 0:
                        eng.wait_ge(cnt.h, val)
                last = fn(eng)
                if inc is not None:
                    assert last is not None
                    last.then_inc(inc.h, inc_k)
            engine_sel(body)
            if inc is not None:
                inc.n += inc_k

        def dma(qsel, out, in_, waits, cnt):
            emit(qsel, waits, lambda e: e.dma_start(out=out, in_=in_), cnt, 16)

        marks = {}

        for nm in _VEC_NAMES:
            dma(SY, vec[nm][:], v_in[nm], [], dsy)
        dma(SY, vec["fb1"][:], v_in["fb1"], [], dsy)
        dma(SY, vec["mask"][:], v_in["mask"], [], dsy)
        setup_loads = dsy.n

        def setup_v(v):
            for i, val in enumerate(_CONSTS):
                v.memset(cpool[:, i:i + 1], val)
            v.memset(ones_bf[:], 1.0)
            v.memset(ones_f32[:], 1.0)
            v.memset(ones_f[:], 1.0)
            return v.engine_nop()
        emit(V, [], setup_v, sv)
        consts_ready = sv.n

        def setup_s(s):
            s.wait_ge(sv.h, consts_ready)
            s.activation(lam_t[:], vec["lam"][:], AF.Exp, scale=-1.0)
            s.activation(logaC[:], lam_t[:], AF.Ln, bias=1.0)
            s.activation(logaC[:], logaC[:], AF.Copy, scale=-0.125)
            s.activation(bq16[:], vec["fb2"][:], AF.Copy)
            return s.activation(bq32[:], bq16[:], AF.Copy)
        emit(S, [(dsy, setup_loads)], setup_s, ss)
        setup_s_done = ss.n

        with ExitStack() as wsc:
            wslot = [sb(f"wslot{i}", [128, KC, D], BF16, wsc)
                     for i in range(4)]
            wA = {"wlin": wslot[0], "wconv": wslot[1],
                  "wa": wslot[2], "wx": wslot[3]}
            wB = {"wlin2": wslot[0], "w1a": wslot[1],
                  "w2a": wslot[2], "w3a": wslot[3]}
            for nm in ["wlin", "wconv", "wa", "wx"]:
                dma(GP, wA[nm][:], wr_in[nm], [], dgp)
            wA_done = {nm: 16 * (i + 1) for i, nm in
                       enumerate(["wlin", "wconv", "wa", "wx"])}

            pha = wsc.enter_context(ExitStack())
            x0t = sb("x0t", [128, KC, TT], F32, pha)
            sqA = sb("sqA", [128, KC, TT], BF16, pha)
            lin_bf = sb("lin_bf", [128, KC, TT], BF16, pha)
            ln1_t = sb("ln1_t", [128, KC, TT], F32, pha)
            a_t = sb("a_t", [128, KC, TT + 1], F32, pha)
            xn_t = sb("xn_t", [128, KC, TT], BF16, pha)
            la_t = sb("la_t", [128, KC, TT], BF16, pha)
            xgel_t = sb("xgel_t", [128, KC, TT], BF16, pha)
            xrg_bf = sb("xrg_bf", [128, KC, TT], BF16, pha)
            it_bf = sb("it_bf", [128, KC, TT], BF16, pha)
            gated_t = sb("gated_t", [128, KC, TT + 1], F32, pha)
            tmpc = sb("tmpcA", [128, TT], F32, pha)
            sgc = sb("sgc", [128, TT], F32, pha)
            rt_c = [sb(f"rt{i}", [128, TT], F32, pha) for i in range(2)]
            h_c = sb("h_cA", [128, TT + 1], F32, pha)
            tmpc2 = h_c[:, 0:TT]

            def f_col0(v):
                for c in range(KC):
                    v.memset(a_t[:, c, 0:1], 0.0)
                    v.memset(gated_t[:, c, 0:1], 0.0)
                return v.engine_nop()
            emit(V, [], f_col0, sv)

            def mm8(wsb, rhs, evbase_mark, per, sem):
                def f(t):
                    for odc in range(KC):
                        if odc >= 2:
                            t.wait_ge(sem.h, marks[evbase_mark] +
                                      per * (odc - 1))
                        last = None
                        for kc in range(KC):
                            last = t.matmul(
                                psM[odc % 2][:],
                                wsb[:, kc, odc * 128:(odc + 1) * 128],
                                rhs[:, kc, :],
                                start=(kc == 0), stop=(kc == KC - 1))
                        last.then_inc(st_.h, 1)
                        st_.n += 1
                    return None
                return f

            def fA_load(j):
                xsl = slice(j * TT, (j + 1) * TT)
                w = [] if j == 0 else [(sv, marks[("ln1", j - 1)])]
                dma(SY, x0t[:], xr_in[:, :, xsl], w, dsy)
                marks[("ldx", j)] = dsy.n

                w = [(dsy, marks[("ldx", j)]), (ss, setup_s_done)]
                if j > 0:
                    w.append((st_, marks[("statsA", j - 1)]))
                emit(S, w, lambda s: s.activation(sqA[:], x0t[:], AF.Square),
                     ss)
                marks[("sqA", j)] = ss.n

                def f_stats(t):
                    for c in range(KC):
                        t.matmul(psS1[0:1, :], ones_f32[:, 0:1], x0t[:, c, :],
                                 start=(c == 0), stop=(c == KC - 1))
                    last = None
                    for c in range(KC):
                        last = t.matmul(psS2[0:1, :], ones_bf[:, 0:1],
                                        sqA[:, c, :], start=(c == 0),
                                        stop=(c == KC - 1))
                    return last
                w = [(dsy, marks[("ldx", j)]), (ss, marks[("sqA", j)]),
                     (sv, consts_ready)]
                if j > 0:
                    w.append((ss, marks[("rowsS", j - 1)]))
                emit(T, w, f_stats, st_)
                marks[("statsA", j)] = st_.n

            def fA_rows(j):
                def f_rows1(s):
                    s.activation(row(0), psS1[0:1, :], AF.Copy,
                                 scale=1.0 / 1024.0)
                    return s.activation(row(1), psS2[0:1, :], AF.Copy,
                                        scale=1.0 / 1024.0)
                w = [(st_, marks[("statsA", j)])]
                if j > 0:
                    w.append((st_, marks[("bcastA", j - 1)]))
                emit(S, w, f_rows1, ss)
                marks[("rowsS", j)] = ss.n

                def f_rows2(v):
                    v.tensor_mul(row(2), row(0), row(0))
                    v.tensor_sub(row(3), row(1), row(2))
                    v.tensor_scalar_max(row(2), row(3), 1e-38)
                    return v.tensor_scalar_add(row(1), row(3), EPS)
                emit(V, [(ss, marks[("rowsS", j)])], f_rows2, sv)

                def f_rows3(s):
                    s.activation(row(4), row(2), AF.Ln)
                    s.activation(row(5), row(1), AF.Ln)
                    return s.activation(row(1), row(5), AF.Exp, scale=-0.5)
                emit(S, [(sv, sv.n)], f_rows3, ss)

                def f_rows4(v):
                    v.tensor_mul(row(2), row(0), row(1))
                    v.tensor_sub(row(3), row(5), row(4))
                    v.tensor_scalar(row(4), row(3), 0.5, -LN1024_HALF,
                                    ALU.mult, ALU.add)
                    return v.tensor_scalar_min(row(3), row(4), 27.631021)
                emit(V, [(ss, ss.n)], f_rows4, sv)

                emit(S, [(sv, sv.n)],
                     lambda s: s.activation(row(4), row(3), AF.Exp,
                                            bias=LN32), ss)
                marks[("rowsS3", j)] = ss.n

            def fA_bc(j):
                xsl = slice(j * TT, (j + 1) * TT)

                def f_bcast(t):
                    t.matmul(psB[:, 0:TT], ones_f[0:1, :], row(1),
                             start=True, stop=True)
                    t.matmul(psB[:, TT:2 * TT], ones_f[0:1, :], row(2),
                             start=True, stop=True)
                    return t.matmul(psB[:, 2 * TT:3 * TT], ones_f[0:1, :],
                                    row(4), start=True, stop=True)
                w = [(ss, marks[("rowsS3", j)])]
                if j > 0:
                    w.append((sv, marks[("ln1", j - 1)]))
                emit(T, w, f_bcast, st_)
                marks[("bcastA", j)] = st_.n

                def f_ln1(v):
                    for c in range(KC):
                        v.tensor_mul(tmpc[:], x0t[:, c, :], psB[:, 0:TT])
                        v.tensor_sub(ln1_t[:, c, :], tmpc[:],
                                     psB[:, TT:2 * TT])
                        v.scalar_tensor_tensor(xn_t[:, c, :],
                                               ln1_t[:, c, :],
                                               vec["g1"][:, c:c + 1],
                                               psB[:, 2 * TT:3 * TT],
                                               ALU.mult, ALU.mult)
                    return v.engine_nop()
                w = [(st_, marks[("bcastA", j)])]
                if j > 0:
                    w += [(dgp, marks[("dma_ln1", j - 1)]),
                          (st_, marks[("linmm", j - 1)])]
                emit(V, w, f_ln1, sv)
                marks[("ln1", j)] = sv.n

                dma(GP, sr_ln1[:, :, xsl], ln1_t[:],
                    [(sv, marks[("ln1", j)])], dgp)
                marks[("dma_ln1", j)] = dgp.n

            def midA1(j):
                xsl = slice(j * TT, (j + 1) * TT)
                marks[("linev_base", j)] = ss.n
                w = [(sv, marks[("ln1", j)]), (dgp, wA_done["wlin"])]
                if j > 0:
                    w.append((ss, marks[("it", j - 1)]))
                emit(T, w, mm8(wA["wlin"], xn_t, ("linev_base", j), 2, ss),
                     None)
                marks[("linmm", j)] = st_.n

                def f_linev(s):
                    base = marks[("linmm", j)] - KC
                    for odc in range(KC):
                        s.wait_ge(st_.h, base + odc + 1)
                        s.activation(lin_bf[:, odc, :], psM[odc % 2][:],
                                     AF.Identity,
                                     bias=vec["blin"][:, odc:odc + 1])
                        last = s.activation(xgel_t[:, odc, :],
                                            psM[odc % 2][:], AF.Gelu,
                                            bias=vec["blin"][:, odc:odc + 1])
                        last.then_inc(ss.h, 2)
                        ss.n += 2
                    return None
                w = []
                if j > 0:
                    w += [(st_, marks[("xrgmm", j - 1)]),
                          (dgp, marks[("dma_xg", j - 1)])]
                emit(S, w, f_linev, None)
                marks[("xgel", j)] = ss.n

                dma(GP, sr_xg[:, :, xsl], xgel_t[:],
                    [(ss, marks[("xgel", j)])], dgp)
                marks[("dma_xg", j)] = dgp.n

                marks[("xrgev_base", j)] = ss.n
                emit(T, [(ss, marks[("xgel", j)]), (dgp, wA_done["wconv"])],
                     mm8(wA["wconv"], lin_bf, ("xrgev_base", j), 1, ss),
                     None)
                marks[("xrgmm", j)] = st_.n

                def f_xrgev(s):
                    base = marks[("xrgmm", j)] - KC
                    for odc in range(KC):
                        s.wait_ge(st_.h, base + odc + 1)
                        last = s.activation(xrg_bf[:, odc, :],
                                            psM[odc % 2][:], AF.Identity,
                                            bias=vec["bconv"][:, odc:odc + 1])
                        last.then_inc(ss.h, 1)
                        ss.n += 1
                    return None
                w = []
                if j > 0:
                    w.append((st_, marks[("itmm", j - 1)]))
                emit(S, w, f_xrgev, None)
                marks[("xrg", j)] = ss.n

            def midA2(j):
                xsl = slice(j * TT, (j + 1) * TT)
                marks[("rtev_base", j)] = ss.n
                emit(T, [(ss, marks[("xrg", j)]), (dgp, wA_done["wa"])],
                     mm8(wA["wa"], xrg_bf, ("rtev_base", j), 1, ss), None)
                marks[("rtmm", j)] = st_.n

                marks[("la_base", j)] = sv.n

                def f_rtev(s):
                    base = marks[("rtmm", j)] - KC
                    for odc in range(KC):
                        s.wait_ge(st_.h, base + odc + 1)
                        if odc >= 2:
                            s.wait_ge(sv.h, marks[("la_base", j)] + odc - 1)
                        last = s.activation(rt_c[odc % 2][:], psM[odc % 2][:],
                                            AF.Sigmoid,
                                            bias=vec["ba"][:, odc:odc + 1])
                        last.then_inc(ss.h, 1)
                        ss.n += 1
                    return None
                w = []
                if j > 0:
                    w.append((dgp, marks[("dma_la", j - 1)]))
                emit(S, w, f_rtev, None)
                marks[("rt", j)] = ss.n

                def f_la(v):
                    base = marks[("rt", j)] - KC
                    for odc in range(KC):
                        v.wait_ge(ss.h, base + odc + 1)
                        last = v.tensor_scalar(la_t[:, odc, :],
                                               rt_c[odc % 2][:],
                                               logaC[:, odc:odc + 1], None,
                                               ALU.mult)
                        last.then_inc(sv.h, 1)
                        sv.n += 1
                    return None
                emit(V, [], f_la, None)
                marks[("la", j)] = sv.n

                dma(GP, sr_la[:, :, xsl], la_t[:],
                    [(sv, marks[("la", j)])], dgp)
                marks[("dma_la", j)] = dgp.n

            def midA3(j):
                xsl = slice(j * TT, (j + 1) * TT)
                marks[("itev_base", j)] = ss.n
                emit(T, [(ss, marks[("rt", j)]), (dgp, wA_done["wx"])],
                     mm8(wA["wx"], xrg_bf, ("itev_base", j), 1, ss), None)
                marks[("itmm", j)] = st_.n

                def f_itev(s):
                    base = marks[("itmm", j)] - KC
                    for odc in range(KC):
                        s.wait_ge(st_.h, base + odc + 1)
                        last = s.activation(it_bf[:, odc, :], psM[odc % 2][:],
                                            AF.Sigmoid,
                                            bias=vec["bx"][:, odc:odc + 1])
                        last.then_inc(ss.h, 1)
                        ss.n += 1
                    return None
                w = []
                if j > 0:
                    w.append((sv, marks[("gated", j - 1)]))
                emit(S, w, f_itev, None)
                marks[("it", j)] = ss.n

            def tailA(j):
                xsl = slice(j * TT, (j + 1) * TT)
                w = [(sv, marks[("la", j)]), (dgp, marks[("dma_la", j)])]
                if j > 0:
                    w.append((sv, marks[("gated", j - 1)]))
                emit(S, w,
                     lambda s: s.activation(a_t[:, :, 1:TT + 1], la_t[:],
                                            AF.Exp), ss)
                marks[("a", j)] = ss.n

                def f_om(v):
                    for c in range(KC):
                        v.tensor_mul(tmpc[:], a_t[:, c, 1:TT + 1],
                                     a_t[:, c, 1:TT + 1])
                        v.tensor_scalar(tmpc2, tmpc[:], -1.0, 1.0,
                                        ALU.mult, ALU.add)
                        v.tensor_scalar_max(gated_t[:, c, 1:TT + 1], tmpc2,
                                            1e-38)
                    return v.engine_nop()
                w = [(ss, marks[("a", j)])]
                if j > 0:
                    w.append((dgp, marks[("dma_g", j - 1)]))
                emit(V, w, f_om, sv)
                marks[("om", j)] = sv.n

                def f_sg(s):
                    for c in range(KC):
                        s.activation(sgc[:], gated_t[:, c, 1:TT + 1], AF.Ln)
                        last = s.activation(gated_t[:, c, 1:TT + 1], sgc[:],
                                            AF.Exp, scale=0.5)
                        last.then_inc(ss.h, 1)
                        ss.n += 1
                    return None
                emit(S, [(sv, marks[("om", j)])], f_sg, None)
                marks[("sg", j)] = ss.n

                def f_gsc(v):
                    for c in range(KC):
                        v.wait_ge(ss.h, marks[("sg", j)] - KC + c + 1)
                        v.tensor_mul(tmpc[:], gated_t[:, c, 1:TT + 1],
                                     it_bf[:, c, :])
                        v.tensor_mul(gated_t[:, c, 1:TT + 1], tmpc[:],
                                     xrg_bf[:, c, :])
                        v.tensor_tensor_scan(h_c[:], a_t[:, c, 0:TT + 1],
                                             gated_t[:, c, 0:TT + 1], 0.0,
                                             ALU.mult, ALU.add) \
                            .then_inc(sv.h, 1)
                        sv.n += 1
                        v.wait_ge(sv.h, sv.n)
                        v.tensor_copy(gated_t[:, c, 0:1], h_c[:, TT:TT + 1])
                        if j == NT - 1:
                            v.tensor_copy(carry[:, c:c + 1],
                                          h_c[:, TT:TT + 1])
                    return v.engine_nop()
                emit(V, [(ss, marks[("it", j)])], f_gsc, sv)
                marks[("gated", j)] = sv.n

                dma(GP, sr_g[:, :, xsl], gated_t[:, :, 1:TT + 1],
                    [(sv, marks[("gated", j)])], dgp)
                marks[("dma_g", j)] = dgp.n

            fA_load(0)
            fA_rows(0)
            fA_bc(0)
            for j in range(NT):
                midA1(j)
                if j + 1 < NT:
                    fA_load(j + 1)
                    fA_rows(j + 1)
                midA2(j)
                if j + 1 < NT:
                    fA_bc(j + 1)
                midA3(j)
                tailA(j)

            dma(SY, cc_in, carry[:], [(sv, marks[("gated", NT - 1)])], dsy)
            cc_up = dsy.n

            emit(GP, [(dsy, cc_up)],
                 lambda g: g.collective_compute(
                     "AllGather", ALU.bypass,
                     replica_groups=[[0, 1], [2, 3], [4, 5], [6, 7]],
                     ins=[cc_in], outs=[cc_out]), ccs)

            dma(SY, hraw[:], cc_out[0], [(ccs, 1)], dsy)
            hraw_ld = dsy.n

            for nm, amark in [("wlin2", ("linmm", NT - 1)),
                              ("w1a", ("xrgmm", NT - 1)),
                              ("w2a", ("rtmm", NT - 1)),
                              ("w3a", ("itmm", NT - 1))]:
                emit(GP, [(st_, marks[amark])],
                     lambda g, nm=nm: g.dma_start(out=wB[nm][:],
                                                  in_=wr_in[nm]), dgp, 16)
                marks[("wB", nm)] = dgp.n

            a_end_st, a_end_ss, a_end_sv = st_.n, ss.n, sv.n
            a_end_dgp = dgp.n
            pha.close()

            phb = wsc.enter_context(ExitStack())
            laB = sb("laB", [128, KC, TT], BF16, phb)
            gatB = sb("gatB", [128, KC, TT + 1], F32, phb)
            xgB = sb("xgB", [128, KC, TT], BF16, phb)
            aB = sb("aB", [128, KC, TT + 1], F32, phb)
            outsB = sb("outsB", [128, KC, TT], BF16, phb)
            gateB = sb("gateB", [128, KC, TT], BF16, phb)
            xghB = sb("xghB", [128, KC, TT], BF16, phb)
            res2 = sb("res2", [128, KC, TT], F32, phb)
            xmB = sb("xmB", [128, KC, TT], BF16, phb)
            combB = sb("combB", [128, KC, TT], BF16, phb)
            hB = sb("hB", [128, TT + 1], F32, phb)
            tvB1 = sb("tvB", [128, TT], F32, phb)
            tvB = [tvB1, tvB1]
            x1B = [sb(f"x1B{i}", [128, TT], F32, phb) for i in range(2)]
            ln1B = sb("ln1B", [128, KC, TT], F32, phb)
            x0B = ln1B

            aend = [(sv, a_end_sv), (ss, a_end_ss), (st_, a_end_st)]

            def f_bcol0(v):
                for c in range(KC):
                    v.memset(aB[:, c, 0:1], 0.0)
                    v.tensor_mul(gatB[:, c, 0:1], hraw[:, c:c + 1],
                                 vec["mask"][:, c:c + 1])
                return v.engine_nop()
            emit(V, [(dsy, hraw_ld)] + aend, f_bcol0, sv)
            bcol0_ready = sv.n

            def frontB(j):
                xsl = slice(j * TT, (j + 1) * TT)
                w = [(dgp, marks[("dma_la", j)])]
                w += aend if j == 0 else [(ss, marks[("aBexp", j - 1)])]
                dma(SY, laB[:], sr_la[:, :, xsl], w, dsy)
                marks[("ld_laB", j)] = dsy.n
                w = [(dgp, marks[("dma_g", j)])]
                w += aend if j == 0 else [(sv, marks[("scanB", j - 1)])]
                dma(SY, gatB[:, :, 1:TT + 1], sr_g[:, :, xsl], w, dsy)
                marks[("ld_gB", j)] = dsy.n
                if j == 0:
                    dma(SY, ln1B[:], sr_ln1[:, :, xsl],
                        [(dgp, marks[("dma_ln1", j)])] + aend, dsy)
                    marks[("ld_ln1B", j)] = dsy.n
                w = [(dgp, marks[("dma_xg", j)])]
                w += aend if j == 0 else [(sv, marks[("scanB", j - 1)])]
                dma(SY, xgB[:], sr_xg[:, :, xsl], w, dsy)
                marks[("ld_xgB", j)] = dsy.n

                w = [(dsy, marks[("ld_laB", j)])]
                if j == 0:
                    w += [(st_, a_end_st)]
                else:
                    w.append((sv, marks[("scanB", j - 1)]))
                emit(S, w, lambda s: s.activation(aB[:, :, 1:TT + 1],
                                                  laB[:], AF.Exp), ss)
                marks[("aBexp", j)] = ss.n

                def f_scan(v):
                    for c in range(KC):
                        v.tensor_tensor_scan(hB[:], aB[:, c, 0:TT + 1],
                                             gatB[:, c, 0:TT + 1], 0.0,
                                             ALU.mult, ALU.add) \
                            .then_inc(sv.h, 1)
                        sv.n += 1
                        v.wait_ge(sv.h, sv.n)
                        v.tensor_mul(xghB[:, c, :], hB[:, 1:TT + 1],
                                     xgB[:, c, :])
                        v.tensor_copy(gatB[:, c, 0:1], hB[:, TT:TT + 1])
                    return v.engine_nop()
                w = [(ss, marks[("aBexp", j)]), (dsy, marks[("ld_gB", j)]),
                     (dsy, marks[("ld_xgB", j)])]
                if j == 0:
                    w.append((sv, bcol0_ready))
                else:
                    w.append((st_, marks[("statsB", j - 1)]))
                emit(V, w, f_scan, sv)
                marks[("scanB", j)] = sv.n

                if DEBUG_TAPS:
                    dma(SY, dbg["d_xgh"][:, :, xsl], xghB[:],
                        [(sv, marks[("scanB", j)])], dsy)

            def mainB(j):
                xsl = slice(j * TT, (j + 1) * TT)
                marks[("res2_base", j)] = sv.n
                w = [(sv, marks[("scanB", j)]), (dgp, marks[("wB", "wlin2")]),
                     (dsy, marks[("ld_ln1B", j)])]
                if j > 0:
                    w.append((sv, marks[("x1", j - 1)]))
                emit(T, w, mm8(wB["wlin2"], xghB, ("res2_base", j), 1, sv),
                     None)
                marks[("xrmm", j)] = st_.n

                def f_res2(v):
                    base = marks[("xrmm", j)] - KC
                    for odc in range(KC):
                        v.wait_ge(st_.h, base + odc + 1)
                        last = v.scalar_tensor_tensor(
                            res2[:, odc, :], psM[odc % 2][:],
                            vec["blin2"][:, odc:odc + 1], ln1B[:, odc, :],
                            ALU.add, ALU.add)
                        last.then_inc(sv.h, 1)
                        sv.n += 1
                    return None
                emit(V, [(dsy, marks[("ld_ln1B", j)])], f_res2, None)
                marks[("res2", j)] = sv.n
                if DEBUG_TAPS:
                    dma(SY, dbg["d_res2"][:, :, xsl], res2[:],
                        [(sv, marks[("res2", j)])], dsy)

                sqB = xghB
                w = [(sv, marks[("res2", j)]), (st_, marks[("xrmm", j)])]
                emit(S, w, lambda s: s.activation(sqB[:], res2[:],
                                                  AF.Square), ss)
                marks[("sqBf", j)] = ss.n

                def f_statsB(t):
                    last = None
                    for c in range(KC):
                        last = t.matmul(psS2[0:1, :], ones_bf[:, 0:1],
                                        sqB[:, c, :], start=(c == 0),
                                        stop=(c == KC - 1))
                    return last
                w = [(ss, marks[("sqBf", j)])]
                if j > 0:
                    w.append((ss, marks[("rowsB1", j - 1)]))
                emit(T, w, f_statsB, st_)
                marks[("statsB", j)] = st_.n

                emit(S, [(st_, marks[("statsB", j)])],
                     lambda s: s.activation(row(0), psS2[0:1, :], AF.Copy),
                     ss)
                marks[("rowsB1", j)] = ss.n

                emit(V, [(ss, marks[("rowsB1", j)])],
                     lambda v: v.tensor_scalar_max(row(1), row(0), 1e-24), sv)
                marks[("rowsB2", j)] = sv.n

                def f_rowsB3(s):
                    s.activation(row(2), row(1), AF.Ln)
                    return s.activation(row(3), row(2), AF.Exp, scale=-0.5,
                                        bias=LN32)
                emit(S, [(sv, marks[("rowsB2", j)])], f_rowsB3, ss)
                marks[("rowsB3", j)] = ss.n

                w = [(ss, marks[("rowsB3", j)])]
                w.append((sv, a_end_sv) if j == 0
                         else (sv, marks[("xm", j - 1)]))
                emit(T, w,
                     lambda t: t.matmul(psB[:, 0:TT], ones_f[0:1, :], row(3),
                                        start=True, stop=True), st_)
                marks[("bcastB", j)] = st_.n

                def f_xm(v):
                    for c in range(KC):
                        v.scalar_tensor_tensor(xmB[:, c, :], res2[:, c, :],
                                               vec["g2"][:, c:c + 1],
                                               psB[:, 0:TT],
                                               ALU.mult, ALU.mult)
                    return v.engine_nop()
                w = [(st_, marks[("bcastB", j)])]
                if j > 0:
                    w.append((st_, marks[("w2mm", j - 1)]))
                emit(V, w, f_xm, sv)
                marks[("xm", j)] = sv.n
                if DEBUG_TAPS:
                    dma(SY, dbg["d_xm"][:, :, xsl], xmB[:],
                        [(sv, marks[("xm", j)])], dsy)

                marks[("outs_base", j)] = ss.n
                emit(T, [(sv, marks[("xm", j)]), (dgp, marks[("wB", "w1a")])],
                     mm8(wB["w1a"], xmB, ("outs_base", j), 1, ss), None)
                marks[("w1mm", j)] = st_.n

                def f_outs(s):
                    base = marks[("w1mm", j)] - KC
                    for odc in range(KC):
                        s.wait_ge(st_.h, base + odc + 1)
                        last = s.activation(outsB[:, odc, :],
                                            psM[odc % 2][:], AF.Sigmoid,
                                            bias=vec["b1"][:, odc:odc + 1])
                        last.then_inc(ss.h, 1)
                        ss.n += 1
                    return None
                w = []
                if j > 0:
                    w.append((ss, marks[("gate", j - 1)]))
                emit(S, w, f_outs, None)
                marks[("outs", j)] = ss.n

                w = [(ss, marks[("outs", j)])]
                if j > 0:
                    w.append((sv, marks[("comb", j - 1)]))
                emit(S, w,
                     lambda s: s.activation(gateB[:], outsB[:], AF.Gelu), ss)
                marks[("gate", j)] = ss.n

            def tailB(j):
                xsl = slice(j * TT, (j + 1) * TT)
                marks[("comb_base", j)] = sv.n
                emit(T, [(ss, marks[("gate", j)]),
                         (dgp, marks[("wB", "w2a")])],
                     mm8(wB["w2a"], xmB, ("comb_base", j), 1, sv), None)
                marks[("w2mm", j)] = st_.n

                def f_comb(v):
                    base = marks[("w2mm", j)] - KC
                    for odc in range(KC):
                        v.wait_ge(st_.h, base + odc + 1)
                        last = v.scalar_tensor_tensor(
                            combB[:, odc, :], psM[odc % 2][:],
                            vec["b2"][:, odc:odc + 1], gateB[:, odc, :],
                            ALU.add, ALU.mult)
                        last.then_inc(sv.h, 1)
                        sv.n += 1
                    return None
                emit(V, [], f_comb, None)
                marks[("comb", j)] = sv.n
                if DEBUG_TAPS:
                    dma(SY, dbg["d_comb"][:, :, xsl], combB[:],
                        [(sv, marks[("comb", j)])], dsy)

                dma(SY, x0B[:], xr_in[:, :, xsl], [(sv, marks[("res2", j)])],
                    dsy)
                marks[("ld_x0B", j)] = dsy.n

                marks[("x1_base", j)] = sv.n
                emit(T, [(sv, marks[("comb", j)]),
                         (dgp, marks[("wB", "w3a")])],
                     mm8(wB["w3a"], combB, ("x1_base", j), 2, sv), None)
                marks[("w3mm", j)] = st_.n

                marks[("x1dma_base", j)] = dgp.n

                def f_x1(v):
                    base = marks[("w3mm", j)] - KC
                    for odc in range(KC):
                        v.wait_ge(st_.h, base + odc + 1)
                        if odc >= 2:
                            v.wait_ge(dgp.h, marks[("x1dma_base", j)] +
                                      16 * (odc - 1))
                        v.scalar_tensor_tensor(tvB[odc % 2][:],
                                               psM[odc % 2][:],
                                               vec["b3"][:, odc:odc + 1],
                                               res2[:, odc, :],
                                               ALU.add, ALU.add)
                        last = v.tensor_add(x1B[odc % 2][:], tvB[odc % 2][:],
                                            x0B[:, odc, :])
                        last.then_inc(sv.h, 2)
                        sv.n += 2
                    return None
                emit(V, [(dsy, marks[("ld_x0B", j)])], f_x1, None)
                marks[("x1", j)] = sv.n

                def f_x1dma(g):
                    base = marks[("x1", j)] - 2 * KC
                    for odc in range(KC):
                        g.wait_ge(sv.h, base + 2 * (odc + 1))
                        last = g.dma_start(out=sr_x1[:, odc, xsl],
                                           in_=x1B[odc % 2][:])
                        last.then_inc(dgp.h, 16)
                        dgp.n += 16
                    return None
                emit(GP, [], f_x1dma, None)
                marks[("x1dma", j)] = dgp.n

                if j + 1 < NT:
                    nsl = slice((j + 1) * TT, (j + 2) * TT)
                    dma(SY, ln1B[:], sr_ln1[:, :, nsl],
                        [(dgp, marks[("dma_ln1", j + 1)]),
                         (sv, marks[("x1", j)])], dsy)
                    marks[("ld_ln1B", j + 1)] = dsy.n

            frontB(0)
            for j in range(NT):
                mainB(j)
                if j + 1 < NT:
                    frontB(j + 1)
                tailB(j)

            b_end_st, b_end_ss, b_end_sv = st_.n, ss.n, sv.n
            b_end_dgp, b_end_dsy = dgp.n, dsy.n
            phb.close()

        with ExitStack() as phc:
            w1sb = sb("w1sb", [128, KC, H], BF16, phc)
            w2sb = sb("w2sb", [128, HC, D], FP8, phc)
            first = True
            for q in range(4):
                w = [(st_, b_end_st), (sv, b_end_sv), (ss, b_end_ss),
                     (dsy, b_end_dsy)] if first else []
                first = False
                emit(GP, w,
                     lambda g, q=q: g.dma_start(
                         out=w1sb[:, :, q * 1024:(q + 1) * 1024],
                         in_=fw1_r[:, :, q * 1024:(q + 1) * 1024]),
                     dgp, 16)
                marks[("w1q", q)] = dgp.n
            emit(GP, [], lambda g: g.dma_start(out=w2sb[:], in_=fw2_r),
                 dgp, 16)
            w2_done = dgp.n

            x1C = [sb(f"x1C{i}", [128, KC, TT], F32, phc) for i in range(2)]
            sqC = sb("sqC", [128, KC, TT], BF16, phc)
            ln2C = [sb(f"ln2C{i}", [128, KC, TT], BF16, phc)
                    for i in range(2)]
            hqC = sb("hqC", [128, HC, TT], FP8, phc)
            o16 = [sb(f"o16_{i}", [128, TT], FP16, phc) for i in range(2)]
            xoC = [sb(f"xoC{i}", [128, TT], F32, phc) for i in range(2)]
            tmpC = sb("tmpC", [128, TT], F32, phc)

            bend = [(sv, b_end_sv), (ss, b_end_ss), (st_, b_end_st)]

            def frontC(j):
                xsl = slice(j * TT, (j + 1) * TT)
                x1t = x1C[j % 2]
                w = [(dgp, marks[("x1dma", j)])]
                if j <= 1:
                    w += bend
                if j > 1:
                    w.append((sv, marks[("xoutV", j - 2)]))
                dma(SY, x1t[:], sr_x1[:, :, xsl], w, dsy)
                marks[("ld_x1C", j)] = dsy.n

                w = [(dsy, marks[("ld_x1C", j)])]
                if j > 0:
                    w.append((st_, marks[("statsC", j - 1)]))
                emit(S, w, lambda s: s.activation(sqC[:], x1t[:],
                                                  AF.Square), ss)
                marks[("sqC", j)] = ss.n

                def f_statsC(t):
                    for c in range(KC):
                        t.matmul(psS1[0:1, :], ones_f32[:, 0:1],
                                 x1t[:, c, :], start=(c == 0),
                                 stop=(c == KC - 1))
                    last = None
                    for c in range(KC):
                        last = t.matmul(psS2[0:1, :], ones_bf[:, 0:1],
                                        sqC[:, c, :], start=(c == 0),
                                        stop=(c == KC - 1))
                    return last
                w = [(dsy, marks[("ld_x1C", j)]), (ss, marks[("sqC", j)])]
                if j > 0:
                    w.append((ss, marks[("rowsC1", j - 1)]))
                emit(T, w, f_statsC, st_)
                marks[("statsC", j)] = st_.n

                def f_rowsC1(s):
                    s.activation(row(0), psS1[0:1, :], AF.Copy,
                                 scale=1.0 / 1024.0)
                    return s.activation(row(1), psS2[0:1, :], AF.Copy,
                                        scale=1.0 / 1024.0)
                emit(S, [(st_, marks[("statsC", j)])], f_rowsC1, ss)
                marks[("rowsC1", j)] = ss.n

                def f_rowsC2(v):
                    v.tensor_mul(row(2), row(0), row(0))
                    v.tensor_sub(row(3), row(1), row(2))
                    return v.tensor_scalar_add(row(2), row(3), EPS)
                emit(V, [(ss, marks[("rowsC1", j)])], f_rowsC2, sv)

                def f_rowsC3(s):
                    s.activation(row(4), row(2), AF.Ln)
                    return s.activation(row(5), row(4), AF.Exp, scale=-0.5)
                emit(S, [(sv, sv.n)], f_rowsC3, ss)

                emit(V, [(ss, ss.n)],
                     lambda v: v.tensor_mul(row(1), row(0), row(5)), sv)
                marks[("rowsC4", j)] = sv.n

                def f_bcastC(t):
                    t.matmul(psB[:, 0:TT], ones_f[0:1, :], row(5),
                             start=True, stop=True)
                    return t.matmul(psB[:, TT:2 * TT], ones_f[0:1, :],
                                    row(1), start=True, stop=True)
                w = [(sv, marks[("rowsC4", j)])]
                if j > 0:
                    w.append((sv, marks[("ln2", j - 1)]))
                emit(T, w, f_bcastC, st_)
                marks[("bcastC", j)] = st_.n

                def f_ln2(v):
                    for c in range(KC):
                        v.tensor_mul(tmpC[:], x1t[:, c, :], psB[:, 0:TT])
                        v.tensor_sub(ln2C[j % 2][:, c, :], tmpC[:],
                                     psB[:, TT:2 * TT])
                    return v.engine_nop()
                w = [(st_, marks[("bcastC", j)])]
                if j > 1:
                    w.append((st_, marks[("w1mmC", j - 2)]))
                emit(V, w, f_ln2, sv)
                marks[("ln2", j)] = sv.n

            def mainC(j):
                xsl = slice(j * TT, (j + 1) * TT)
                x1t = x1C[j % 2]
                marks[("hq_base", j)] = ss.n

                def f_w1(t):
                    for hc in range(HC):
                        if hc >= 2:
                            t.wait_ge(ss.h, marks[("hq_base", j)] + hc - 1)
                        if hc == 8:
                            t.wait_ge(dgp.h, marks[("w1q", 1)])
                        if hc == 16:
                            t.wait_ge(dgp.h, marks[("w1q", 2)])
                        if hc == 24:
                            t.wait_ge(dgp.h, marks[("w1q", 3)])
                        last = None
                        for kc in range(KC):
                            last = t.matmul(
                                psM[hc % 2][:],
                                w1sb[:, kc, hc * 128:(hc + 1) * 128],
                                ln2C[j % 2][:, kc, :],
                                start=(kc == 0), stop=(kc == KC - 1))
                        last.then_inc(st_.h, 1)
                        st_.n += 1
                    return None
                w = [(sv, marks[("ln2", j)]), (dgp, marks[("w1q", 0)])]
                if j > 0:
                    w.append((ss, marks[("o16", j - 1)]))
                emit(T, w, f_w1, None)
                marks[("w1mmC", j)] = st_.n

                def f_hq(s):
                    base = marks[("w1mmC", j)] - HC
                    for hc in range(HC):
                        s.wait_ge(st_.h, base + hc + 1)
                        last = s.activation(hqC[:, hc, :], psM[hc % 2][:],
                                            AF.Relu,
                                            bias=vec["fb1"][:, hc:hc + 1])
                        last.then_inc(ss.h, 1)
                        ss.n += 1
                    return None
                w = []
                if j > 0:
                    w.append((st_, marks[("w2mmC", j - 1)]))
                emit(S, w, f_hq, None)
                marks[("hq", j)] = ss.n

            def tailC(j):
                xsl = slice(j * TT, (j + 1) * TT)
                x1t = x1C[j % 2]
                marks[("o16_base", j)] = ss.n

                def f_w2(t):
                    for odc in range(KC):
                        if odc >= 2:
                            t.wait_ge(ss.h, marks[("o16_base", j)] + odc - 1)
                        last = None
                        for hc in range(HC):
                            last = t.matmul(
                                psM[odc % 2][:],
                                w2sb[:, hc, odc * 128:(odc + 1) * 128],
                                hqC[:, hc, :],
                                start=(hc == 0), stop=(hc == HC - 1))
                        last.then_inc(st_.h, 1)
                        st_.n += 1
                    return None
                emit(T, [(ss, marks[("hq", j)]), (dgp, w2_done)], f_w2, None)
                marks[("w2mmC", j)] = st_.n

                marks[("xoutV_base", j)] = sv.n

                def f_o16(s):
                    base = marks[("w2mmC", j)] - KC
                    for odc in range(KC):
                        s.wait_ge(st_.h, base + odc + 1)
                        if odc >= 2:
                            s.wait_ge(sv.h, marks[("xoutV_base", j)] +
                                      odc - 1)
                        last = s.activation(o16[odc % 2][:], psM[odc % 2][:],
                                            AF.Identity,
                                            bias=bq32[:, odc:odc + 1])
                        last.then_inc(ss.h, 1)
                        ss.n += 1
                    return None
                emit(S, [], f_o16, None)
                marks[("o16", j)] = ss.n

                marks[("xodma_base", j)] = dsy.n

                def f_xout(v):
                    base = marks[("o16", j)] - KC
                    for odc in range(KC):
                        v.wait_ge(ss.h, base + odc + 1)
                        if odc >= 2:
                            v.wait_ge(dsy.h, marks[("xodma_base", j)] +
                                      16 * (odc - 1))
                        last = v.tensor_add(xoC[odc % 2][:], o16[odc % 2][:],
                                            x1t[:, odc, :])
                        last.then_inc(sv.h, 1)
                        sv.n += 1
                    return None
                emit(V, [], f_xout, None)
                marks[("xoutV", j)] = sv.n

                def f_xodma(sync):
                    base = marks[("xoutV", j)] - KC
                    for odc in range(KC):
                        sync.wait_ge(sv.h, base + odc + 1)
                        last = sync.dma_start(out=xr_out[:, odc, xsl],
                                              in_=xoC[odc % 2][:])
                        last.then_inc(dsy.h, 16)
                        dsy.n += 16
                    return None
                emit(SY, [], f_xodma, None)
                marks[("xodma", j)] = dsy.n

            frontC(0)
            frontC(1)
            for j in range(NT):
                mainC(j)
                tailC(j)
                if j + 2 < NT:
                    frontC(j + 2)

            emit(SY, [(dsy, dsy.n)], lambda e: None, None)

    return nc


_CACHE = {}


def _get_prog():
    if "nc" not in _CACHE:
        _CACHE["nc"] = _build()
    return _CACHE["nc"]


def kernel(**inputs):
    inp = {k: np.asarray(v) for k, v in inputs.items()}

    def vstage(v, kc):
        v = np.asarray(v, np.float32)
        return np.ascontiguousarray(v.reshape(kc, 128).T)

    def wt(name):
        return np.ascontiguousarray(np.asarray(inp[name], np.float32).T)

    shared = {
        "wlin": wt("L0_attn_W_lin"), "wconv": wt("L0_attn_W_conv"),
        "wa": wt("L0_attn_Wa"), "wx": wt("L0_attn_Wx"),
        "wlin2": wt("L0_attn_W_lin2"), "w1a": wt("L0_attn_W1"),
        "w2a": wt("L0_attn_W2"), "w3a": wt("L0_attn_W3"),
        "fw1": wt("L0_ffn_w1"), "fw2": wt("L0_ffn_w2"),
        "vg1": vstage(inp["L0_attn_g1"], KC),
        "vg2": vstage(inp["L0_attn_g2"], KC),
        "vblin": vstage(inp["L0_attn_b_lin"], KC),
        "vbconv": vstage(inp["L0_attn_b_conv"], KC),
        "vba": vstage(inp["L0_attn_ba"], KC),
        "vbx": vstage(inp["L0_attn_bx"], KC),
        "vblin2": vstage(inp["L0_attn_b_lin2"], KC),
        "vb1": vstage(inp["L0_attn_b1"], KC),
        "vb2": vstage(inp["L0_attn_b2"], KC),
        "vb3": vstage(inp["L0_attn_b3"], KC),
        "vlam": vstage(inp["L0_attn_Lam"], KC),
        "vfb2": vstage(inp["L0_ffn_b2"], KC),
        "vfb1": vstage(inp["L0_ffn_b1"], HC),
    }

    x = np.asarray(inp["L0_attn_x"], np.float32)
    in_maps = []
    for c in range(NCORE):
        b, half = c // 2, c % 2
        t0 = half * TPC
        m = dict(shared)
        m["x"] = np.ascontiguousarray(x[b, t0:t0 + TPC, :].T)
        m["vmask"] = np.full((128, KC), float(half), np.float32)
        in_maps.append(m)

    nc = _get_prog()
    res = run_bass_kernel_spmd(nc, in_maps, core_ids=list(range(NCORE)))

    out = np.empty((4, 4096, D), np.float32)
    for c in range(NCORE):
        b, half = c // 2, c % 2
        t0 = half * TPC
        out[b, t0:t0 + TPC, :] = res.results[c]["xout"].T
    return out


# version 09
# speedup vs baseline: 1.2658x; 1.0698x over previous
import numpy as np
import concourse.bass as bass
import concourse.mybir as mybir
from concourse.bass_utils import run_bass_kernel_spmd
from contextlib import ExitStack

F32 = mybir.dt.float32
BF16 = mybir.dt.bfloat16
FP16 = mybir.dt.float16
FP8 = mybir.dt.float8e4
AF = mybir.ActivationFunctionType
ALU = mybir.AluOpType

D = 1024
KC = 8
TPC = 2048
NT = 4
TT = 512
H = 4096
HC = 32
NCORE = 8
EPS = 1e-5
LN32 = float(np.log(32.0))
LN1024_HALF = 0.5 * float(np.log(1024.0))

_CONSTS = [0.0, 1.0, -1.0, 0.5, -0.5, -0.125, 1.0 / 1024.0, LN32]

DEBUG_TAPS = False

_VEC_NAMES = ["g1", "g2", "blin", "bconv", "ba", "bx", "blin2", "b1", "b2",
              "b3", "lam", "fb2"]


class _Cnt:
    def __init__(self, handle):
        self.h = handle
        self.n = 0


def _build():
    nc = bass.Bass("TRN2", target_bir_lowering=False, debug=False)

    x_in = nc.dram_tensor("x", [D, TPC], F32, kind="ExternalInput").ap()
    w_in = {}
    for nm in ["wlin", "wconv", "wa", "wx", "wlin2", "w1a", "w2a", "w3a"]:
        w_in[nm] = nc.dram_tensor(nm, [D, D], F32, kind="ExternalInput").ap()
    fw1_in = nc.dram_tensor("fw1", [D, H], F32, kind="ExternalInput").ap()
    fw2_in = nc.dram_tensor("fw2", [H, D], F32, kind="ExternalInput").ap()
    v_in = {}
    for nm in _VEC_NAMES:
        v_in[nm] = nc.dram_tensor("v" + nm, [128, KC], F32,
                                  kind="ExternalInput").ap()
    v_in["fb1"] = nc.dram_tensor("vfb1", [128, HC], F32,
                                 kind="ExternalInput").ap()
    v_in["mask"] = nc.dram_tensor("vmask", [128, KC], F32,
                                  kind="ExternalInput").ap()

    x_out = nc.dram_tensor("xout", [D, TPC], F32, kind="ExternalOutput").ap()

    dbg = {}
    if DEBUG_TAPS:
        for nm, dt_ in [("d_xgh", BF16), ("d_res2", F32), ("d_xm", BF16),
                        ("d_comb", BF16)]:
            dbg[nm] = nc.dram_tensor(nm, [D, TPC], dt_,
                                     kind="ExternalOutput").ap() \
                .rearrange("(c p) t -> p c t", p=128)

    s_ln1 = nc.dram_tensor("s_ln1", [D, TPC], F32, kind="Internal").ap()
    s_la = nc.dram_tensor("s_la", [D, TPC], BF16, kind="Internal").ap()
    s_g = nc.dram_tensor("s_g", [D, TPC], F32, kind="Internal").ap()
    s_xg = nc.dram_tensor("s_xg", [D, TPC], BF16, kind="Internal").ap()
    s_x1 = nc.dram_tensor("s_x1", [D, TPC], F32, kind="Internal").ap()
    cc_in = nc.dram_tensor("cc_in", [128, KC], F32, kind="Internal").ap()
    cc_out = nc.dram_tensor("cc_out", [2, 128, KC], F32, kind="Internal").ap()

    xr_in = x_in.rearrange("(c p) t -> p c t", p=128)
    xr_out = x_out.rearrange("(c p) t -> p c t", p=128)
    wr_in = {nm: a.rearrange("(c p) o -> p c o", p=128)
             for nm, a in w_in.items()}
    fw1_r = fw1_in.rearrange("(c p) o -> p c o", p=128)
    fw2_r = fw2_in.rearrange("(c p) o -> p c o", p=128)
    sr_ln1 = s_ln1.rearrange("(c p) t -> p c t", p=128)
    sr_la = s_la.rearrange("(c p) t -> p c t", p=128)
    sr_g = s_g.rearrange("(c p) t -> p c t", p=128)
    sr_xg = s_xg.rearrange("(c p) t -> p c t", p=128)
    sr_x1 = s_x1.rearrange("(c p) t -> p c t", p=128)

    with ExitStack() as top:
        def sb(name, shape, dt_, stack):
            return stack.enter_context(nc.sbuf_tensor(name, shape, dt_))

        cpool = sb("cpool", [128, len(_CONSTS)], F32, top)
        for i, val in enumerate(_CONSTS):
            nc.const_aps.register(F32, val, cpool[:, i:i + 1])
        ones_bf = sb("ones_bf", [128, 1], BF16, top)
        ones_f32 = sb("ones_f32", [128, 1], F32, top)
        ones_f = sb("ones_f", [1, 128], F32, top)
        vec = {nm: sb("sv_" + nm, [128, KC], F32, top) for nm in _VEC_NAMES}
        vec["fb1"] = sb("sv_fb1", [128, HC], F32, top)
        vec["mask"] = sb("sv_mask", [128, KC], F32, top)
        logaC = sb("logaC", [128, KC], F32, top)
        lam_t = sb("lam_t", [128, KC], F32, top)
        bq16 = sb("bq16", [128, KC], FP16, top)
        bq32 = sb("bq32", [128, KC], F32, top)
        carry = sb("carry", [128, KC], F32, top)
        hraw = sb("hraw", [128, KC], F32, top)
        rows_t = sb("rows_t", [1, 6 * TT], F32, top)

        def row(i):
            return rows_t[0:1, i * TT:(i + 1) * TT]

        psS1 = top.enter_context(nc.psum_tensor("psS1", [128, TT], F32))
        psS2 = top.enter_context(nc.psum_tensor("psS2", [128, TT], F32))
        psB = top.enter_context(nc.psum_tensor("psB", [128, 3 * TT], F32))
        psM0 = top.enter_context(nc.psum_tensor("psM0", [128, TT], F32))
        psM1 = top.enter_context(nc.psum_tensor("psM1", [128, TT], F32))
        psM = [psM0, psM1]

        sems = {nm: _Cnt(top.enter_context(nc.semaphore(nm)))
                for nm in ["dsy", "dgp", "sv", "ss", "st", "cc"]}
        dsy, dgp, sv, ss, st_, ccs = (sems[n] for n in
                                      ["dsy", "dgp", "sv", "ss", "st", "cc"])

        block = top.enter_context(nc.Block())
        SY, GP, V, S, T = (block.sync, block.gpsimd, block.vector,
                           block.scalar, block.tensor)

        def emit(engine_sel, waits, fn, inc=None, inc_k=1):
            def body(eng):
                for cnt, val in waits:
                    if val > 0:
                        eng.wait_ge(cnt.h, val)
                last = fn(eng)
                if inc is not None:
                    assert last is not None
                    last.then_inc(inc.h, inc_k)
            engine_sel(body)
            if inc is not None:
                inc.n += inc_k

        def dma(qsel, out, in_, waits, cnt):
            emit(qsel, waits, lambda e: e.dma_start(out=out, in_=in_), cnt, 16)

        marks = {}

        for nm in _VEC_NAMES:
            dma(SY, vec[nm][:], v_in[nm], [], dsy)
        dma(SY, vec["fb1"][:], v_in["fb1"], [], dsy)
        dma(SY, vec["mask"][:], v_in["mask"], [], dsy)
        setup_loads = dsy.n

        def setup_v(v):
            for i, val in enumerate(_CONSTS):
                v.memset(cpool[:, i:i + 1], val)
            v.memset(ones_bf[:], 1.0)
            v.memset(ones_f32[:], 1.0)
            v.memset(ones_f[:], 1.0)
            return v.engine_nop()
        emit(V, [], setup_v, sv)
        consts_ready = sv.n

        def setup_s(s):
            s.wait_ge(sv.h, consts_ready)
            s.activation(lam_t[:], vec["lam"][:], AF.Exp, scale=-1.0)
            s.activation(logaC[:], lam_t[:], AF.Ln, bias=1.0)
            s.activation(logaC[:], logaC[:], AF.Copy, scale=-0.125)
            s.activation(bq16[:], vec["fb2"][:], AF.Copy)
            return s.activation(bq32[:], bq16[:], AF.Copy)
        emit(S, [(dsy, setup_loads)], setup_s, ss)
        setup_s_done = ss.n

        with ExitStack() as wsc:
            wslot = [sb(f"wslot{i}", [128, KC, D], BF16, wsc)
                     for i in range(4)]
            wA = {"wlin": wslot[0], "wconv": wslot[1],
                  "wa": wslot[2], "wx": wslot[3]}
            wB = {"wlin2": wslot[0], "w1a": wslot[1],
                  "w2a": wslot[2], "w3a": wslot[3]}
            for nm in ["wlin", "wconv", "wa", "wx"]:
                dma(GP, wA[nm][:], wr_in[nm], [], dgp)
            wA_done = {nm: 16 * (i + 1) for i, nm in
                       enumerate(["wlin", "wconv", "wa", "wx"])}

            pha = wsc.enter_context(ExitStack())
            x0t = sb("x0t", [128, KC, TT], F32, pha)
            sqA = sb("sqA", [128, KC, TT], BF16, pha)
            lin_bf = sb("lin_bf", [128, KC, TT], BF16, pha)
            ln1_t = sb("ln1_t", [128, KC, TT], F32, pha)
            a_t = sb("a_t", [128, KC, TT + 1], F32, pha)
            xn_t = sb("xn_t", [128, KC, TT], BF16, pha)
            la_t = sb("la_t", [128, KC, TT], BF16, pha)
            xgel_t = sb("xgel_t", [128, KC, TT], BF16, pha)
            xrg_bf = sb("xrg_bf", [128, KC, TT], BF16, pha)
            it_bf = sb("it_bf", [128, KC, TT], BF16, pha)
            gated_t = sb("gated_t", [128, KC, TT + 1], F32, pha)
            tmpc = sb("tmpcA", [128, TT], F32, pha)
            sgc = sb("sgc", [128, TT], F32, pha)
            rt_c = [sb(f"rt{i}", [128, TT], F32, pha) for i in range(2)]
            h_c = sb("h_cA", [128, TT + 1], F32, pha)
            tmpc2 = h_c[:, 0:TT]

            def f_col0(v):
                for c in range(KC):
                    v.memset(a_t[:, c, 0:1], 0.0)
                    v.memset(gated_t[:, c, 0:1], 0.0)
                return v.engine_nop()
            emit(V, [], f_col0, sv)

            def mm8(wsb, rhs, evbase_mark, per, sem):
                def f(t):
                    for odc in range(KC):
                        if odc >= 2:
                            t.wait_ge(sem.h, marks[evbase_mark] +
                                      per * (odc - 1))
                        last = None
                        for kc in range(KC):
                            last = t.matmul(
                                psM[odc % 2][:],
                                wsb[:, kc, odc * 128:(odc + 1) * 128],
                                rhs[:, kc, :],
                                start=(kc == 0), stop=(kc == KC - 1))
                        last.then_inc(st_.h, 1)
                        st_.n += 1
                    return None
                return f

            def fA_load(j):
                xsl = slice(j * TT, (j + 1) * TT)
                w = [] if j == 0 else [(sv, marks[("ln1", j - 1)])]
                dma(SY, x0t[:], xr_in[:, :, xsl], w, dsy)
                marks[("ldx", j)] = dsy.n

                w = [(dsy, marks[("ldx", j)]), (ss, setup_s_done)]
                if j > 0:
                    w.append((st_, marks[("statsA", j - 1)]))
                emit(S, w, lambda s: s.activation(sqA[:], x0t[:], AF.Square),
                     ss)
                marks[("sqA", j)] = ss.n

                def f_stats(t):
                    for c in range(KC):
                        t.matmul(psS1[0:1, :], ones_f32[:, 0:1], x0t[:, c, :],
                                 start=(c == 0), stop=(c == KC - 1))
                    last = None
                    for c in range(KC):
                        last = t.matmul(psS2[0:1, :], ones_bf[:, 0:1],
                                        sqA[:, c, :], start=(c == 0),
                                        stop=(c == KC - 1))
                    return last
                w = [(dsy, marks[("ldx", j)]), (ss, marks[("sqA", j)]),
                     (sv, consts_ready)]
                if j > 0:
                    w.append((ss, marks[("rowsS", j - 1)]))
                emit(T, w, f_stats, st_)
                marks[("statsA", j)] = st_.n

            def fA_rows(j):
                def f_rows1(s):
                    s.activation(row(0), psS1[0:1, :], AF.Copy,
                                 scale=1.0 / 1024.0)
                    return s.activation(row(1), psS2[0:1, :], AF.Copy,
                                        scale=1.0 / 1024.0)
                w = [(st_, marks[("statsA", j)])]
                if j > 0:
                    w.append((st_, marks[("bcastA", j - 1)]))
                emit(S, w, f_rows1, ss)
                marks[("rowsS", j)] = ss.n

                def f_rows2(v):
                    v.tensor_mul(row(2), row(0), row(0))
                    v.tensor_sub(row(3), row(1), row(2))
                    v.tensor_scalar_max(row(2), row(3), 1e-38)
                    return v.tensor_scalar_add(row(1), row(3), EPS)
                emit(V, [(ss, marks[("rowsS", j)])], f_rows2, sv)

                def f_rows3(s):
                    s.activation(row(4), row(2), AF.Ln)
                    s.activation(row(5), row(1), AF.Ln)
                    return s.activation(row(1), row(5), AF.Exp, scale=-0.5)
                emit(S, [(sv, sv.n)], f_rows3, ss)

                def f_rows4(v):
                    v.tensor_mul(row(2), row(0), row(1))
                    v.tensor_sub(row(3), row(5), row(4))
                    v.tensor_scalar(row(4), row(3), 0.5, -LN1024_HALF,
                                    ALU.mult, ALU.add)
                    return v.tensor_scalar_min(row(3), row(4), 27.631021)
                emit(V, [(ss, ss.n)], f_rows4, sv)

                emit(S, [(sv, sv.n)],
                     lambda s: s.activation(row(4), row(3), AF.Exp,
                                            bias=LN32), ss)
                marks[("rowsS3", j)] = ss.n

            def fA_bc(j):
                xsl = slice(j * TT, (j + 1) * TT)

                def f_bcast(t):
                    t.matmul(psB[:, 0:TT], ones_f[0:1, :], row(1),
                             start=True, stop=True)
                    t.matmul(psB[:, TT:2 * TT], ones_f[0:1, :], row(2),
                             start=True, stop=True)
                    return t.matmul(psB[:, 2 * TT:3 * TT], ones_f[0:1, :],
                                    row(4), start=True, stop=True)
                w = [(ss, marks[("rowsS3", j)])]
                if j > 0:
                    w.append((sv, marks[("ln1", j - 1)]))
                emit(T, w, f_bcast, st_)
                marks[("bcastA", j)] = st_.n

                def f_ln1(v):
                    for c in range(KC):
                        v.tensor_mul(tmpc[:], x0t[:, c, :], psB[:, 0:TT])
                        v.tensor_sub(ln1_t[:, c, :], tmpc[:],
                                     psB[:, TT:2 * TT])
                        v.scalar_tensor_tensor(xn_t[:, c, :],
                                               ln1_t[:, c, :],
                                               vec["g1"][:, c:c + 1],
                                               psB[:, 2 * TT:3 * TT],
                                               ALU.mult, ALU.mult)
                    return v.engine_nop()
                w = [(st_, marks[("bcastA", j)])]
                if j > 0:
                    w += [(dgp, marks[("dma_ln1", j - 1)]),
                          (st_, marks[("linmm", j - 1)])]
                emit(V, w, f_ln1, sv)
                marks[("ln1", j)] = sv.n

                dma(GP, sr_ln1[:, :, xsl], ln1_t[:],
                    [(sv, marks[("ln1", j)])], dgp)
                marks[("dma_ln1", j)] = dgp.n

            def midA1(j):
                xsl = slice(j * TT, (j + 1) * TT)
                marks[("linev_base", j)] = ss.n
                w = [(sv, marks[("ln1", j)]), (dgp, wA_done["wlin"])]
                if j > 0:
                    w.append((ss, marks[("it", j - 1)]))
                emit(T, w, mm8(wA["wlin"], xn_t, ("linev_base", j), 2, ss),
                     None)
                marks[("linmm", j)] = st_.n

                def f_linev(s):
                    base = marks[("linmm", j)] - KC
                    for odc in range(KC):
                        s.wait_ge(st_.h, base + odc + 1)
                        s.activation(lin_bf[:, odc, :], psM[odc % 2][:],
                                     AF.Identity,
                                     bias=vec["blin"][:, odc:odc + 1])
                        last = s.activation(xgel_t[:, odc, :],
                                            psM[odc % 2][:], AF.Gelu,
                                            bias=vec["blin"][:, odc:odc + 1])
                        last.then_inc(ss.h, 2)
                        ss.n += 2
                    return None
                w = []
                if j > 0:
                    w += [(st_, marks[("xrgmm", j - 1)]),
                          (dgp, marks[("dma_xg", j - 1)])]
                emit(S, w, f_linev, None)
                marks[("xgel", j)] = ss.n

                dma(GP, sr_xg[:, :, xsl], xgel_t[:],
                    [(ss, marks[("xgel", j)])], dgp)
                marks[("dma_xg", j)] = dgp.n

                marks[("xrgev_base", j)] = ss.n
                emit(T, [(ss, marks[("xgel", j)]), (dgp, wA_done["wconv"])],
                     mm8(wA["wconv"], lin_bf, ("xrgev_base", j), 1, ss),
                     None)
                marks[("xrgmm", j)] = st_.n

                def f_xrgev(s):
                    base = marks[("xrgmm", j)] - KC
                    for odc in range(KC):
                        s.wait_ge(st_.h, base + odc + 1)
                        last = s.activation(xrg_bf[:, odc, :],
                                            psM[odc % 2][:], AF.Identity,
                                            bias=vec["bconv"][:, odc:odc + 1])
                        last.then_inc(ss.h, 1)
                        ss.n += 1
                    return None
                w = []
                if j > 0:
                    w.append((st_, marks[("itmm", j - 1)]))
                emit(S, w, f_xrgev, None)
                marks[("xrg", j)] = ss.n

            def midA2(j):
                xsl = slice(j * TT, (j + 1) * TT)
                marks[("rtev_base", j)] = ss.n
                emit(T, [(ss, marks[("xrg", j)]), (dgp, wA_done["wa"])],
                     mm8(wA["wa"], xrg_bf, ("rtev_base", j), 1, ss), None)
                marks[("rtmm", j)] = st_.n

                marks[("la_base", j)] = sv.n

                def f_rtev(s):
                    base = marks[("rtmm", j)] - KC
                    for odc in range(KC):
                        s.wait_ge(st_.h, base + odc + 1)
                        if odc >= 2:
                            s.wait_ge(sv.h, marks[("la_base", j)] + odc - 1)
                        last = s.activation(rt_c[odc % 2][:], psM[odc % 2][:],
                                            AF.Sigmoid,
                                            bias=vec["ba"][:, odc:odc + 1])
                        last.then_inc(ss.h, 1)
                        ss.n += 1
                    return None
                w = []
                if j > 0:
                    w.append((dgp, marks[("dma_la", j - 1)]))
                emit(S, w, f_rtev, None)
                marks[("rt", j)] = ss.n

                def f_la(v):
                    base = marks[("rt", j)] - KC
                    for odc in range(KC):
                        v.wait_ge(ss.h, base + odc + 1)
                        last = v.tensor_scalar(la_t[:, odc, :],
                                               rt_c[odc % 2][:],
                                               logaC[:, odc:odc + 1], None,
                                               ALU.mult)
                        last.then_inc(sv.h, 1)
                        sv.n += 1
                    return None
                emit(V, [], f_la, None)
                marks[("la", j)] = sv.n

                dma(GP, sr_la[:, :, xsl], la_t[:],
                    [(sv, marks[("la", j)])], dgp)
                marks[("dma_la", j)] = dgp.n

            def midA3(j):
                xsl = slice(j * TT, (j + 1) * TT)
                marks[("itev_base", j)] = ss.n
                emit(T, [(ss, marks[("rt", j)]), (dgp, wA_done["wx"])],
                     mm8(wA["wx"], xrg_bf, ("itev_base", j), 1, ss), None)
                marks[("itmm", j)] = st_.n

                def f_itev(s):
                    base = marks[("itmm", j)] - KC
                    for odc in range(KC):
                        s.wait_ge(st_.h, base + odc + 1)
                        last = s.activation(it_bf[:, odc, :], psM[odc % 2][:],
                                            AF.Sigmoid,
                                            bias=vec["bx"][:, odc:odc + 1])
                        last.then_inc(ss.h, 1)
                        ss.n += 1
                    return None
                w = []
                if j > 0:
                    w.append((sv, marks[("gated", j - 1)]))
                emit(S, w, f_itev, None)
                marks[("it", j)] = ss.n

            def tailA(j):
                xsl = slice(j * TT, (j + 1) * TT)
                w = [(sv, marks[("la", j)]), (dgp, marks[("dma_la", j)])]
                if j > 0:
                    w.append((sv, marks[("gated", j - 1)]))
                emit(S, w,
                     lambda s: s.activation(a_t[:, :, 1:TT + 1], la_t[:],
                                            AF.Exp), ss)
                marks[("a", j)] = ss.n

                def f_om(v):
                    for c in range(KC):
                        v.tensor_mul(tmpc[:], a_t[:, c, 1:TT + 1],
                                     a_t[:, c, 1:TT + 1])
                        v.tensor_scalar(tmpc2, tmpc[:], -1.0, 1.0,
                                        ALU.mult, ALU.add)
                        v.tensor_scalar_max(gated_t[:, c, 1:TT + 1], tmpc2,
                                            1e-38)
                    return v.engine_nop()
                w = [(ss, marks[("a", j)])]
                if j > 0:
                    w.append((dgp, marks[("dma_g", j - 1)]))
                emit(V, w, f_om, sv)
                marks[("om", j)] = sv.n

                def f_sg(s):
                    for c in range(KC):
                        s.activation(sgc[:], gated_t[:, c, 1:TT + 1], AF.Ln)
                        last = s.activation(gated_t[:, c, 1:TT + 1], sgc[:],
                                            AF.Exp, scale=0.5)
                        last.then_inc(ss.h, 1)
                        ss.n += 1
                    return None
                emit(S, [(sv, marks[("om", j)])], f_sg, None)
                marks[("sg", j)] = ss.n

                def f_gsc(v):
                    for c in range(KC):
                        v.wait_ge(ss.h, marks[("sg", j)] - KC + c + 1)
                        v.tensor_mul(tmpc[:], gated_t[:, c, 1:TT + 1],
                                     it_bf[:, c, :])
                        v.tensor_mul(gated_t[:, c, 1:TT + 1], tmpc[:],
                                     xrg_bf[:, c, :])
                        v.tensor_tensor_scan(h_c[:], a_t[:, c, 0:TT + 1],
                                             gated_t[:, c, 0:TT + 1], 0.0,
                                             ALU.mult, ALU.add) \
                            .then_inc(sv.h, 1)
                        sv.n += 1
                        v.wait_ge(sv.h, sv.n)
                        v.tensor_copy(gated_t[:, c, 0:1], h_c[:, TT:TT + 1])
                        if j == NT - 1:
                            v.tensor_copy(carry[:, c:c + 1],
                                          h_c[:, TT:TT + 1])
                    return v.engine_nop()
                emit(V, [(ss, marks[("it", j)])], f_gsc, sv)
                marks[("gated", j)] = sv.n

                dma(GP, sr_g[:, :, xsl], gated_t[:, :, 1:TT + 1],
                    [(sv, marks[("gated", j)])], dgp)
                marks[("dma_g", j)] = dgp.n

            fA_load(0)
            fA_rows(0)
            fA_bc(0)
            for j in range(NT):
                midA1(j)
                if j + 1 < NT:
                    fA_load(j + 1)
                    fA_rows(j + 1)
                midA2(j)
                if j + 1 < NT:
                    fA_bc(j + 1)
                midA3(j)
                tailA(j)

            dma(SY, cc_in, carry[:], [(sv, marks[("gated", NT - 1)])], dsy)
            cc_up = dsy.n

            emit(GP, [(dsy, cc_up)],
                 lambda g: g.collective_compute(
                     "AllGather", ALU.bypass,
                     replica_groups=[[0, 1], [2, 3], [4, 5], [6, 7]],
                     ins=[cc_in], outs=[cc_out]), ccs)

            dma(SY, hraw[:], cc_out[0], [(ccs, 1)], dsy)
            hraw_ld = dsy.n

            for nm, amark in [("wlin2", ("linmm", NT - 1)),
                              ("w1a", ("xrgmm", NT - 1)),
                              ("w2a", ("rtmm", NT - 1)),
                              ("w3a", ("itmm", NT - 1))]:
                emit(GP, [(st_, marks[amark])],
                     lambda g, nm=nm: g.dma_start(out=wB[nm][:],
                                                  in_=wr_in[nm]), dgp, 16)
                marks[("wB", nm)] = dgp.n

            a_end_st, a_end_ss, a_end_sv = st_.n, ss.n, sv.n
            a_end_dgp = dgp.n
            pha.close()

            phb = wsc.enter_context(ExitStack())
            laB = sb("laB", [128, KC, TT], BF16, phb)
            gatB = sb("gatB", [128, KC, TT + 1], F32, phb)
            xgB = sb("xgB", [128, KC, TT], BF16, phb)
            aB = sb("aB", [128, KC, TT + 1], F32, phb)
            outsB = sb("outsB", [128, KC, TT], BF16, phb)
            gateB = sb("gateB", [128, KC, TT], BF16, phb)
            xghB = sb("xghB", [128, KC, TT], BF16, phb)
            res2 = sb("res2", [128, KC, TT], F32, phb)
            xmB = sb("xmB", [128, KC, TT], BF16, phb)
            combB = sb("combB", [128, KC, TT], BF16, phb)
            hB = sb("hB", [128, TT + 1], F32, phb)
            tvB1 = sb("tvB", [128, TT], F32, phb)
            tvB = [tvB1, tvB1]
            x1B = [sb(f"x1B{i}", [128, TT], F32, phb) for i in range(2)]
            ln1B = sb("ln1B", [128, KC, TT], F32, phb)
            x0B = ln1B

            aend = [(sv, a_end_sv), (ss, a_end_ss), (st_, a_end_st)]

            def f_bcol0(v):
                for c in range(KC):
                    v.memset(aB[:, c, 0:1], 0.0)
                    v.tensor_mul(gatB[:, c, 0:1], hraw[:, c:c + 1],
                                 vec["mask"][:, c:c + 1])
                return v.engine_nop()
            emit(V, [(dsy, hraw_ld)] + aend, f_bcol0, sv)
            bcol0_ready = sv.n

            def frontB(j):
                xsl = slice(j * TT, (j + 1) * TT)
                w = [(dgp, marks[("dma_la", j)])]
                w += aend if j == 0 else [(ss, marks[("aBexp", j - 1)])]
                dma(SY, laB[:], sr_la[:, :, xsl], w, dsy)
                marks[("ld_laB", j)] = dsy.n
                w = [(dgp, marks[("dma_g", j)])]
                w += aend if j == 0 else [(sv, marks[("scanB", j - 1)])]
                dma(SY, gatB[:, :, 1:TT + 1], sr_g[:, :, xsl], w, dsy)
                marks[("ld_gB", j)] = dsy.n
                if j == 0:
                    dma(SY, ln1B[:], sr_ln1[:, :, xsl],
                        [(dgp, marks[("dma_ln1", j)])] + aend, dsy)
                    marks[("ld_ln1B", j)] = dsy.n
                w = [(dgp, marks[("dma_xg", j)])]
                w += aend if j == 0 else [(sv, marks[("scanB", j - 1)])]
                dma(SY, xgB[:], sr_xg[:, :, xsl], w, dsy)
                marks[("ld_xgB", j)] = dsy.n

                w = [(dsy, marks[("ld_laB", j)])]
                if j == 0:
                    w += [(st_, a_end_st)]
                else:
                    w.append((sv, marks[("scanB", j - 1)]))
                emit(S, w, lambda s: s.activation(aB[:, :, 1:TT + 1],
                                                  laB[:], AF.Exp), ss)
                marks[("aBexp", j)] = ss.n

                def f_scan(v):
                    for c in range(KC):
                        v.tensor_tensor_scan(hB[:], aB[:, c, 0:TT + 1],
                                             gatB[:, c, 0:TT + 1], 0.0,
                                             ALU.mult, ALU.add) \
                            .then_inc(sv.h, 1)
                        sv.n += 1
                        v.wait_ge(sv.h, sv.n)
                        v.tensor_mul(xghB[:, c, :], hB[:, 1:TT + 1],
                                     xgB[:, c, :])
                        v.tensor_copy(gatB[:, c, 0:1], hB[:, TT:TT + 1])
                    return v.engine_nop()
                w = [(ss, marks[("aBexp", j)]), (dsy, marks[("ld_gB", j)]),
                     (dsy, marks[("ld_xgB", j)])]
                if j == 0:
                    w.append((sv, bcol0_ready))
                else:
                    w.append((st_, marks[("statsB", j - 1)]))
                emit(V, w, f_scan, sv)
                marks[("scanB", j)] = sv.n

                if DEBUG_TAPS:
                    dma(SY, dbg["d_xgh"][:, :, xsl], xghB[:],
                        [(sv, marks[("scanB", j)])], dsy)

            def mainB(j):
                xsl = slice(j * TT, (j + 1) * TT)
                marks[("res2_base", j)] = sv.n
                w = [(sv, marks[("scanB", j)]), (dgp, marks[("wB", "wlin2")]),
                     (dsy, marks[("ld_ln1B", j)])]
                if j > 0:
                    w.append((sv, marks[("x1", j - 1)]))
                emit(T, w, mm8(wB["wlin2"], xghB, ("res2_base", j), 1, sv),
                     None)
                marks[("xrmm", j)] = st_.n

                def f_res2(v):
                    base = marks[("xrmm", j)] - KC
                    for odc in range(KC):
                        v.wait_ge(st_.h, base + odc + 1)
                        last = v.scalar_tensor_tensor(
                            res2[:, odc, :], psM[odc % 2][:],
                            vec["blin2"][:, odc:odc + 1], ln1B[:, odc, :],
                            ALU.add, ALU.add)
                        last.then_inc(sv.h, 1)
                        sv.n += 1
                    return None
                emit(V, [(dsy, marks[("ld_ln1B", j)])], f_res2, None)
                marks[("res2", j)] = sv.n
                if DEBUG_TAPS:
                    dma(SY, dbg["d_res2"][:, :, xsl], res2[:],
                        [(sv, marks[("res2", j)])], dsy)

                sqB = xghB
                w = [(sv, marks[("res2", j)]), (st_, marks[("xrmm", j)])]
                emit(S, w, lambda s: s.activation(sqB[:], res2[:],
                                                  AF.Square), ss)
                marks[("sqBf", j)] = ss.n

                def f_statsB(t):
                    last = None
                    for c in range(KC):
                        last = t.matmul(psS2[0:1, :], ones_bf[:, 0:1],
                                        sqB[:, c, :], start=(c == 0),
                                        stop=(c == KC - 1))
                    return last
                w = [(ss, marks[("sqBf", j)])]
                if j > 0:
                    w.append((ss, marks[("rowsB1", j - 1)]))
                emit(T, w, f_statsB, st_)
                marks[("statsB", j)] = st_.n

                emit(S, [(st_, marks[("statsB", j)])],
                     lambda s: s.activation(row(0), psS2[0:1, :], AF.Copy),
                     ss)
                marks[("rowsB1", j)] = ss.n

                emit(V, [(ss, marks[("rowsB1", j)])],
                     lambda v: v.tensor_scalar_max(row(1), row(0), 1e-24), sv)
                marks[("rowsB2", j)] = sv.n

                def f_rowsB3(s):
                    s.activation(row(2), row(1), AF.Ln)
                    return s.activation(row(3), row(2), AF.Exp, scale=-0.5,
                                        bias=LN32)
                emit(S, [(sv, marks[("rowsB2", j)])], f_rowsB3, ss)
                marks[("rowsB3", j)] = ss.n

                w = [(ss, marks[("rowsB3", j)])]
                w.append((sv, a_end_sv) if j == 0
                         else (sv, marks[("xm", j - 1)]))
                emit(T, w,
                     lambda t: t.matmul(psB[:, 0:TT], ones_f[0:1, :], row(3),
                                        start=True, stop=True), st_)
                marks[("bcastB", j)] = st_.n

                def f_xm(v):
                    for c in range(KC):
                        v.scalar_tensor_tensor(xmB[:, c, :], res2[:, c, :],
                                               vec["g2"][:, c:c + 1],
                                               psB[:, 0:TT],
                                               ALU.mult, ALU.mult)
                    return v.engine_nop()
                w = [(st_, marks[("bcastB", j)])]
                if j > 0:
                    w.append((st_, marks[("w2mm", j - 1)]))
                emit(V, w, f_xm, sv)
                marks[("xm", j)] = sv.n
                if DEBUG_TAPS:
                    dma(SY, dbg["d_xm"][:, :, xsl], xmB[:],
                        [(sv, marks[("xm", j)])], dsy)

                marks[("outs_base", j)] = ss.n
                emit(T, [(sv, marks[("xm", j)]), (dgp, marks[("wB", "w1a")])],
                     mm8(wB["w1a"], xmB, ("outs_base", j), 1, ss), None)
                marks[("w1mm", j)] = st_.n

                def f_outs(s):
                    base = marks[("w1mm", j)] - KC
                    for odc in range(KC):
                        s.wait_ge(st_.h, base + odc + 1)
                        last = s.activation(outsB[:, odc, :],
                                            psM[odc % 2][:], AF.Sigmoid,
                                            bias=vec["b1"][:, odc:odc + 1])
                        last.then_inc(ss.h, 1)
                        ss.n += 1
                    return None
                w = []
                if j > 0:
                    w.append((ss, marks[("gate", j - 1)]))
                emit(S, w, f_outs, None)
                marks[("outs", j)] = ss.n

                w = [(ss, marks[("outs", j)])]
                if j > 0:
                    w.append((sv, marks[("comb", j - 1)]))
                emit(S, w,
                     lambda s: s.activation(gateB[:], outsB[:], AF.Gelu), ss)
                marks[("gate", j)] = ss.n

            def tailB(j):
                xsl = slice(j * TT, (j + 1) * TT)
                marks[("comb_base", j)] = sv.n
                emit(T, [(ss, marks[("gate", j)]),
                         (dgp, marks[("wB", "w2a")])],
                     mm8(wB["w2a"], xmB, ("comb_base", j), 1, sv), None)
                marks[("w2mm", j)] = st_.n

                def f_comb(v):
                    base = marks[("w2mm", j)] - KC
                    for odc in range(KC):
                        v.wait_ge(st_.h, base + odc + 1)
                        last = v.scalar_tensor_tensor(
                            combB[:, odc, :], psM[odc % 2][:],
                            vec["b2"][:, odc:odc + 1], gateB[:, odc, :],
                            ALU.add, ALU.mult)
                        last.then_inc(sv.h, 1)
                        sv.n += 1
                    return None
                emit(V, [], f_comb, None)
                marks[("comb", j)] = sv.n
                if DEBUG_TAPS:
                    dma(SY, dbg["d_comb"][:, :, xsl], combB[:],
                        [(sv, marks[("comb", j)])], dsy)

                dma(SY, x0B[:], xr_in[:, :, xsl], [(sv, marks[("res2", j)])],
                    dsy)
                marks[("ld_x0B", j)] = dsy.n

                marks[("x1_base", j)] = sv.n
                emit(T, [(sv, marks[("comb", j)]),
                         (dgp, marks[("wB", "w3a")])],
                     mm8(wB["w3a"], combB, ("x1_base", j), 2, sv), None)
                marks[("w3mm", j)] = st_.n

                marks[("x1dma_base", j)] = dgp.n

                def f_x1(v):
                    base = marks[("w3mm", j)] - KC
                    for odc in range(KC):
                        v.wait_ge(st_.h, base + odc + 1)
                        if odc >= 2:
                            v.wait_ge(dgp.h, marks[("x1dma_base", j)] +
                                      16 * (odc - 1))
                        v.scalar_tensor_tensor(tvB[odc % 2][:],
                                               psM[odc % 2][:],
                                               vec["b3"][:, odc:odc + 1],
                                               res2[:, odc, :],
                                               ALU.add, ALU.add)
                        last = v.tensor_add(x1B[odc % 2][:], tvB[odc % 2][:],
                                            x0B[:, odc, :])
                        last.then_inc(sv.h, 2)
                        sv.n += 2
                    return None
                emit(V, [(dsy, marks[("ld_x0B", j)])], f_x1, None)
                marks[("x1", j)] = sv.n

                def f_x1dma(g):
                    base = marks[("x1", j)] - 2 * KC
                    for odc in range(KC):
                        g.wait_ge(sv.h, base + 2 * (odc + 1))
                        last = g.dma_start(out=sr_x1[:, odc, xsl],
                                           in_=x1B[odc % 2][:])
                        last.then_inc(dgp.h, 16)
                        dgp.n += 16
                    return None
                emit(GP, [], f_x1dma, None)
                marks[("x1dma", j)] = dgp.n

                if j + 1 < NT:
                    nsl = slice((j + 1) * TT, (j + 2) * TT)
                    dma(SY, ln1B[:], sr_ln1[:, :, nsl],
                        [(dgp, marks[("dma_ln1", j + 1)]),
                         (sv, marks[("x1", j)])], dsy)
                    marks[("ld_ln1B", j + 1)] = dsy.n

            frontB(0)
            for j in range(NT):
                mainB(j)
                if j + 1 < NT:
                    frontB(j + 1)
                tailB(j)

            b_end_st, b_end_ss, b_end_sv = st_.n, ss.n, sv.n
            b_end_dgp, b_end_dsy = dgp.n, dsy.n
            phb.close()

        with ExitStack() as phc:
            w1sb = sb("w1sb", [128, KC, H], BF16, phc)
            w2sb = sb("w2sb", [128, HC, D], FP8, phc)
            first = True
            for q in range(4):
                w = [(st_, b_end_st), (sv, b_end_sv), (ss, b_end_ss),
                     (dsy, b_end_dsy)] if first else []
                first = False
                emit(GP, w,
                     lambda g, q=q: g.dma_start(
                         out=w1sb[:, :, q * 1024:(q + 1) * 1024],
                         in_=fw1_r[:, :, q * 1024:(q + 1) * 1024]),
                     dgp, 16)
                marks[("w1q", q)] = dgp.n
            emit(GP, [], lambda g: g.dma_start(out=w2sb[:], in_=fw2_r),
                 dgp, 16)
            w2_done = dgp.n

            x1C = [sb(f"x1C{i}", [128, KC, TT], F32, phc) for i in range(2)]
            sqC = sb("sqC", [128, KC, TT], BF16, phc)
            ln2C = [sb(f"ln2C{i}", [128, KC, TT], BF16, phc)
                    for i in range(2)]
            hqC = sb("hqC", [128, HC, TT], FP8, phc)
            o16 = [sb(f"o16_{i}", [128, TT], FP16, phc) for i in range(2)]
            xoC = [sb(f"xoC{i}", [128, TT], F32, phc) for i in range(2)]
            tmpC = sb("tmpC", [128, TT], F32, phc)

            bend = [(sv, b_end_sv), (ss, b_end_ss), (st_, b_end_st)]

            def frontC(j):
                xsl = slice(j * TT, (j + 1) * TT)
                x1t = x1C[j % 2]
                w = [(dgp, marks[("x1dma", j)])]
                if j <= 1:
                    w += bend
                if j > 1:
                    w.append((sv, marks[("xoutV", j - 2)]))
                dma(SY, x1t[:], sr_x1[:, :, xsl], w, dsy)
                marks[("ld_x1C", j)] = dsy.n

                w = [(dsy, marks[("ld_x1C", j)])]
                if j > 0:
                    w.append((st_, marks[("statsC", j - 1)]))
                emit(S, w, lambda s: s.activation(sqC[:], x1t[:],
                                                  AF.Square), ss)
                marks[("sqC", j)] = ss.n

                def f_statsC(t):
                    for c in range(KC):
                        t.matmul(psS1[0:1, :], ones_f32[:, 0:1],
                                 x1t[:, c, :], start=(c == 0),
                                 stop=(c == KC - 1))
                    last = None
                    for c in range(KC):
                        last = t.matmul(psS2[0:1, :], ones_bf[:, 0:1],
                                        sqC[:, c, :], start=(c == 0),
                                        stop=(c == KC - 1))
                    return last
                w = [(dsy, marks[("ld_x1C", j)]), (ss, marks[("sqC", j)])]
                if j > 0:
                    w.append((ss, marks[("rowsC1", j - 1)]))
                emit(T, w, f_statsC, st_)
                marks[("statsC", j)] = st_.n

                def f_rowsC1(s):
                    s.activation(row(0), psS1[0:1, :], AF.Copy,
                                 scale=1.0 / 1024.0)
                    return s.activation(row(1), psS2[0:1, :], AF.Copy,
                                        scale=1.0 / 1024.0)
                emit(S, [(st_, marks[("statsC", j)])], f_rowsC1, ss)
                marks[("rowsC1", j)] = ss.n

                def f_rowsC2(v):
                    v.tensor_mul(row(2), row(0), row(0))
                    v.tensor_sub(row(3), row(1), row(2))
                    return v.tensor_scalar_add(row(2), row(3), EPS)
                emit(V, [(ss, marks[("rowsC1", j)])], f_rowsC2, sv)

                def f_rowsC3(s):
                    s.activation(row(4), row(2), AF.Ln)
                    return s.activation(row(5), row(4), AF.Exp, scale=-0.5)
                emit(S, [(sv, sv.n)], f_rowsC3, ss)

                emit(V, [(ss, ss.n)],
                     lambda v: v.tensor_mul(row(1), row(0), row(5)), sv)
                marks[("rowsC4", j)] = sv.n

                def f_bcastC(t):
                    t.matmul(psB[:, 0:TT], ones_f[0:1, :], row(5),
                             start=True, stop=True)
                    return t.matmul(psB[:, TT:2 * TT], ones_f[0:1, :],
                                    row(1), start=True, stop=True)
                w = [(sv, marks[("rowsC4", j)])]
                if j > 0:
                    w.append((sv, marks[("ln2", j - 1)]))
                emit(T, w, f_bcastC, st_)
                marks[("bcastC", j)] = st_.n

                def f_ln2(v):
                    for c in range(KC):
                        v.tensor_mul(tmpC[:], x1t[:, c, :], psB[:, 0:TT])
                        v.tensor_sub(ln2C[j % 2][:, c, :], tmpC[:],
                                     psB[:, TT:2 * TT])
                    return v.engine_nop()
                w = [(st_, marks[("bcastC", j)])]
                if j > 1:
                    w.append((st_, marks[("w1mmC", j - 2)]))
                emit(V, w, f_ln2, sv)
                marks[("ln2", j)] = sv.n

            def mainC(j):
                xsl = slice(j * TT, (j + 1) * TT)
                x1t = x1C[j % 2]
                marks[("hq_base", j)] = ss.n

                def f_w1(t):
                    for hc in range(HC):
                        if hc >= 2:
                            t.wait_ge(ss.h, marks[("hq_base", j)] + hc - 1)
                        if hc == 8:
                            t.wait_ge(dgp.h, marks[("w1q", 1)])
                        if hc == 16:
                            t.wait_ge(dgp.h, marks[("w1q", 2)])
                        if hc == 24:
                            t.wait_ge(dgp.h, marks[("w1q", 3)])
                        last = None
                        for kc in range(KC):
                            last = t.matmul(
                                psM[hc % 2][:],
                                w1sb[:, kc, hc * 128:(hc + 1) * 128],
                                ln2C[j % 2][:, kc, :],
                                start=(kc == 0), stop=(kc == KC - 1))
                        last.then_inc(st_.h, 1)
                        st_.n += 1
                    return None
                w = [(sv, marks[("ln2", j)]), (dgp, marks[("w1q", 0)])]
                if j > 0:
                    w.append((ss, marks[("o16", j - 1)]))
                emit(T, w, f_w1, None)
                marks[("w1mmC", j)] = st_.n

                def f_hq(s):
                    base = marks[("w1mmC", j)] - HC
                    for hc in range(HC):
                        s.wait_ge(st_.h, base + hc + 1)
                        last = s.activation(hqC[:, hc, :], psM[hc % 2][:],
                                            AF.Relu,
                                            bias=vec["fb1"][:, hc:hc + 1])
                        last.then_inc(ss.h, 1)
                        ss.n += 1
                    return None
                w = []
                if j > 0:
                    w.append((st_, marks[("w2mmC", j - 1)]))
                emit(S, w, f_hq, None)
                marks[("hq", j)] = ss.n

            def tailC(j):
                xsl = slice(j * TT, (j + 1) * TT)
                x1t = x1C[j % 2]
                marks[("o16_base", j)] = ss.n

                def f_w2(t):
                    for odc in range(KC):
                        if odc >= 2:
                            t.wait_ge(ss.h, marks[("o16_base", j)] + odc - 1)
                        last = None
                        for ck in range(HC // 2):
                            last = t.matmul(
                                psM[odc % 2][:],
                                w2sb[:, 2 * ck:2 * ck + 2,
                                     odc * 128:(odc + 1) * 128],
                                hqC[:, 2 * ck:2 * ck + 2, :],
                                start=(ck == 0), stop=(ck == HC // 2 - 1),
                                perf_mode=mybir.MatmulPerfMode.DoubleRow)
                        last.then_inc(st_.h, 1)
                        st_.n += 1
                    return None
                emit(T, [(ss, marks[("hq", j)]), (dgp, w2_done)], f_w2, None)
                marks[("w2mmC", j)] = st_.n

                marks[("xoutV_base", j)] = sv.n

                def f_o16(s):
                    base = marks[("w2mmC", j)] - KC
                    for odc in range(KC):
                        s.wait_ge(st_.h, base + odc + 1)
                        if odc >= 2:
                            s.wait_ge(sv.h, marks[("xoutV_base", j)] +
                                      odc - 1)
                        last = s.activation(o16[odc % 2][:], psM[odc % 2][:],
                                            AF.Identity,
                                            bias=bq32[:, odc:odc + 1])
                        last.then_inc(ss.h, 1)
                        ss.n += 1
                    return None
                emit(S, [], f_o16, None)
                marks[("o16", j)] = ss.n

                marks[("xodma_base", j)] = dsy.n

                def f_xout(v):
                    base = marks[("o16", j)] - KC
                    for odc in range(KC):
                        v.wait_ge(ss.h, base + odc + 1)
                        if odc >= 2:
                            v.wait_ge(dsy.h, marks[("xodma_base", j)] +
                                      16 * (odc - 1))
                        last = v.tensor_add(xoC[odc % 2][:], o16[odc % 2][:],
                                            x1t[:, odc, :])
                        last.then_inc(sv.h, 1)
                        sv.n += 1
                    return None
                emit(V, [], f_xout, None)
                marks[("xoutV", j)] = sv.n

                def f_xodma(sync):
                    base = marks[("xoutV", j)] - KC
                    for odc in range(KC):
                        sync.wait_ge(sv.h, base + odc + 1)
                        last = sync.dma_start(out=xr_out[:, odc, xsl],
                                              in_=xoC[odc % 2][:])
                        last.then_inc(dsy.h, 16)
                        dsy.n += 16
                    return None
                emit(SY, [], f_xodma, None)
                marks[("xodma", j)] = dsy.n

            frontC(0)
            frontC(1)
            for j in range(NT):
                mainC(j)
                tailC(j)
                if j + 2 < NT:
                    frontC(j + 2)

            emit(SY, [(dsy, dsy.n)], lambda e: None, None)

    return nc


_CACHE = {}


def _get_prog():
    if "nc" not in _CACHE:
        _CACHE["nc"] = _build()
    return _CACHE["nc"]


def kernel(**inputs):
    inp = {k: np.asarray(v) for k, v in inputs.items()}

    def vstage(v, kc):
        v = np.asarray(v, np.float32)
        return np.ascontiguousarray(v.reshape(kc, 128).T)

    def wt(name):
        return np.ascontiguousarray(np.asarray(inp[name], np.float32).T)

    shared = {
        "wlin": wt("L0_attn_W_lin"), "wconv": wt("L0_attn_W_conv"),
        "wa": wt("L0_attn_Wa"), "wx": wt("L0_attn_Wx"),
        "wlin2": wt("L0_attn_W_lin2"), "w1a": wt("L0_attn_W1"),
        "w2a": wt("L0_attn_W2"), "w3a": wt("L0_attn_W3"),
        "fw1": wt("L0_ffn_w1"), "fw2": wt("L0_ffn_w2"),
        "vg1": vstage(inp["L0_attn_g1"], KC),
        "vg2": vstage(inp["L0_attn_g2"], KC),
        "vblin": vstage(inp["L0_attn_b_lin"], KC),
        "vbconv": vstage(inp["L0_attn_b_conv"], KC),
        "vba": vstage(inp["L0_attn_ba"], KC),
        "vbx": vstage(inp["L0_attn_bx"], KC),
        "vblin2": vstage(inp["L0_attn_b_lin2"], KC),
        "vb1": vstage(inp["L0_attn_b1"], KC),
        "vb2": vstage(inp["L0_attn_b2"], KC),
        "vb3": vstage(inp["L0_attn_b3"], KC),
        "vlam": vstage(inp["L0_attn_Lam"], KC),
        "vfb2": vstage(inp["L0_ffn_b2"], KC),
        "vfb1": vstage(inp["L0_ffn_b1"], HC),
    }

    x = np.asarray(inp["L0_attn_x"], np.float32)
    in_maps = []
    for c in range(NCORE):
        b, half = c // 2, c % 2
        t0 = half * TPC
        m = dict(shared)
        m["x"] = np.ascontiguousarray(x[b, t0:t0 + TPC, :].T)
        m["vmask"] = np.full((128, KC), float(half), np.float32)
        in_maps.append(m)

    nc = _get_prog()
    res = run_bass_kernel_spmd(nc, in_maps, core_ids=list(range(NCORE)))

    out = np.empty((4, 4096, D), np.float32)
    for c in range(NCORE):
        b, half = c // 2, c % 2
        t0 = half * TPC
        out[b, t0:t0 + TPC, :] = res.results[c]["xout"].T
    return out
